# version 14
# baseline (speedup 1.0000x reference)
import numpy as np
import ml_dtypes
from contextlib import ExitStack

import concourse.bass as bass
import concourse.mybir as mybir
from concourse.bass_utils import run_bass_kernel_spmd

F32 = mybir.dt.float32
BF16 = mybir.dt.bfloat16
AF = mybir.ActivationFunctionType
ALU = mybir.AluOpType

P = 128
TOK = 2048
NTT = 4
TTS = 512
FC = 8
HC = 32
D = 1024
DH = 4096
NH = 16
PRI = 6144
XBB = 6
NXB = 8
NPB = 48
EPS = 1e-5

ENGS = ("tensor", "vector", "scalar", "gpsimd", "sync")


class Prog:
    def __init__(self):
        self.q = {e: [] for e in ENGS}

    def emit(self, eng, fn):
        self.q[eng].append(fn)

    def run(self, block):
        for eng in ENGS:
            fns = self.q[eng]
            if not fns:
                continue
            getattr(block, eng)(lambda e, fns=fns: [fn(e) for fn in fns])


class Sem:
    def __init__(self, stack, nc, name):
        self.h = stack.enter_context(nc.semaphore(name))
        self.n = 0


def build_program():
    nc = bass.Bass("TRN2", target_bir_lowering=False, detect_race_conditions=False)

    x_in = nc.dram_tensor("x_in", [P, FC * TOK], F32, kind="ExternalInput")
    xp_in = nc.dram_tensor("xp_in", [PRI, D], F32, kind="ExternalInput")
    beta_r = nc.dram_tensor("beta_r", [1, NH], F32, kind="ExternalInput")
    wq_in = nc.dram_tensor("wq_in", [FC, P, D], F32, kind="ExternalInput")
    wv_in = nc.dram_tensor("wv_in", [FC, P, D], F32, kind="ExternalInput")
    wg_in = nc.dram_tensor("wg_in", [FC, P, D], F32, kind="ExternalInput")
    wo_in = nc.dram_tensor("wo_in", [FC, P, D], F32, kind="ExternalInput")
    gam_in = nc.dram_tensor("gam_in", [P, FC], F32, kind="ExternalInput")
    bet_in = nc.dram_tensor("bet_in", [P, FC], F32, kind="ExternalInput")
    w12_in = nc.dram_tensor("w12_in", [HC, P, 2048], F32, kind="ExternalInput")
    h1_in = nc.dram_tensor("h1_in", [HC, P, DH], F32, kind="ExternalInput")
    h2_in = nc.dram_tensor("h2_in", [HC, P, DH], F32, kind="ExternalInput")
    w3_in = nc.dram_tensor("w3_in", [FC, P, DH], F32, kind="ExternalInput")
    s1_in = nc.dram_tensor("s1_in", [P, HC], F32, kind="ExternalInput")
    s2_in = nc.dram_tensor("s2_in", [P, HC], F32, kind="ExternalInput")
    sh1_in = nc.dram_tensor("sh1_in", [P, HC], F32, kind="ExternalInput")
    sh2_in = nc.dram_tensor("sh2_in", [P, HC], F32, kind="ExternalInput")
    s3_in = nc.dram_tensor("s3_in", [P, FC], F32, kind="ExternalInput")

    out_fm = nc.dram_tensor("out_fm", [P, FC * TOK], F32, kind="ExternalOutput")
    attn_sp = nc.dram_tensor("attn_sp", [P, FC * TOK], F32, kind="Internal")
    xn_sp = nc.dram_tensor("xn_sp", [P, FC * TOK], BF16, kind="Internal")

    ctx = ExitStack()
    E = ctx.enter_context

    s1_t = E(nc.sbuf_tensor("s1_t", [P, HC], F32))
    s2_t = E(nc.sbuf_tensor("s2_t", [P, HC], F32))
    sh1_t = E(nc.sbuf_tensor("sh1_t", [P, HC], F32))
    sh2_t = E(nc.sbuf_tensor("sh2_t", [P, HC], F32))
    s3_t = E(nc.sbuf_tensor("s3_t", [P, FC], F32))
    gam_t = E(nc.sbuf_tensor("gam_t", [P, FC], F32))
    bet_t = E(nc.sbuf_tensor("bet_t", [P, FC], F32))
    beta_sb = E(nc.sbuf_tensor("beta_sb", [1, NH], F32))
    rows = E(nc.sbuf_tensor("rows", [1, 128], F32))
    erow = E(nc.sbuf_tensor("erow", [1, P], F32))
    onesr = E(nc.sbuf_tensor("onesr", [1, P], F32))
    onesc_bf = E(nc.sbuf_tensor("onesc_bf", [P, 1], BF16))
    omg_pre = E(nc.sbuf_tensor("omg_pre", [P, NPB * NH], BF16))
    omg_b = E(nc.sbuf_tensor("omg_b", [P, 4, XBB * NH], BF16))
    mu_all = E(nc.sbuf_tensor("mu_all", [P, NPB], BF16))
    rstd_all = E(nc.sbuf_tensor("rstd_all", [P, NPB], F32))
    bnAll = E(nc.sbuf_tensor("bnAll", [P, XBB, 12], F32))
    mv6 = E(nc.sbuf_tensor("mv6", [P, 2 * XBB], F32))
    var6 = E(nc.sbuf_tensor("var6", [P, XBB], F32))
    wadj_bf = E(nc.sbuf_tensor("wadj_bf", [NH, D], BF16))
    wadjT = E(nc.sbuf_tensor("wadjT", [P, FC * NH], BF16))
    carry_fm = E(nc.sbuf_tensor("carry_fm", [P, FC], F32))
    lam_fm = E(nc.sbuf_tensor("lam_fm", [P, FC], F32))
    w12b = E(nc.sbuf_tensor("w12b", [P, 2, 2048], BF16))
    wsl = E(nc.sbuf_tensor("wsl", [P, 2, D], BF16))
    PS = E(nc.psum_tensor("PS", [P, 4096], F32))

    xn_cm = nc.sbuf_tensor("xn", [P, FC, TOK], BF16)
    xn = xn_cm.__enter__()
    arow_cm = nc.sbuf_tensor("arow", [1, 1024], F32)
    arow = arow_cm.__enter__()
    brow_cm = nc.sbuf_tensor("brow", [1, 1024], F32)
    brow = brow_cm.__enter__()
    mrow_cm = nc.sbuf_tensor("mrow", [1, 1024], F32)
    mrow = mrow_cm.__enter__()

    sstk = ExitStack()
    ctx.enter_context(sstk)

    def mks(name):
        return Sem(sstk, nc, name)

    ld = mks("ld")
    ldx = mks("ldx")
    gwv = mks("gwv")
    xp = mks("xp")
    vpre = mks("vpre")
    gio = mks("gio")
    peT1 = mks("peT1")
    somg = mks("somg")
    vlam = mks("vlam")
    vst = mks("vst")
    sst = mks("sst")
    vomg = mks("vomg")
    pewsum = mks("pewsum")
    vwadj = mks("vwadj")
    vwm = mks("vwm")
    vch = mks("vch")
    tps = mks("tps")
    peo3 = mks("peo3")
    vco3 = mks("vco3")
    vxb = mks("vxb")
    vxq = mks("vxq")
    sxq = mks("sxq")
    peE2 = mks("peE2")
    peSum = mks("peSum")
    vev = mks("vev")
    sAB = mks("sAB")
    vll = mks("vll")
    vrb = mks("vrb")
    vE1 = mks("vE1")
    sE1 = mks("sE1")
    sE2 = mks("sE2")
    gllb = mks("gllb")
    gE2 = mks("gE2")
    vones = mks("vones")
    peSS = mks("peSS")
    vrow1 = mks("vrow1")
    srow = mks("srow")
    vrow2 = mks("vrow2")
    peAB = mks("peAB")
    vxn = mks("vxn")
    barA = mks("barA")
    peV = mks("peV")
    vsc = mks("vsc")
    peQ = mks("peQ")
    vy = mks("vy")
    vys = mks("vys")
    peSSy = mks("peSSy")
    vyr1 = mks("vyr1")
    syr = mks("syr")
    vyr2 = mks("vyr2")
    peABy = mks("peABy")
    vab = mks("vab")
    peG = mks("peG")
    sgs = mks("sgs")
    vy2 = mks("vy2")
    peWo = mks("peWo")
    sWo = mks("sWo")
    spl = mks("spl")
    sxn = mks("sxn")
    barB = mks("barB")
    gxn = mks("gxn")
    gw12 = mks("gw12")
    peFU = mks("peFU")
    sU = mks("sU")
    vU = mks("vU")
    gh = mks("gh")
    peFHh = mks("peFHh")
    sH = mks("sH")
    sax = mks("sax")
    vO = mks("vO")
    sout = mks("sout")

    PRG = [Prog(), Prog(), Prog()]

    def W(blk, eng, sem, val=None):
        v = sem.n if val is None else val
        if v <= 0:
            return
        PRG[blk].emit(eng, lambda e, s=sem.h, v=v: e.wait_ge(s, v))

    def OP(blk, eng, fn, inc=None, inck=1):
        if inc is not None:
            inc.n += inck
        def run(e, fn=fn, inc=inc, inck=inck):
            inst = fn(e)
            if inc is not None:
                inst.then_inc(inc.h, inck)
            return inst
        PRG[blk].emit(eng, run)

    def VOP(blk, fn):
        OP(blk, "vector", fn, inc=vch)
        W(blk, "vector", vch)

    def DMA(blk, eng, out_ap, in_ap, inc):
        OP(blk, eng, lambda e, o=out_ap, i=in_ap: e.dma_start(out=o, in_=i),
           inc=inc, inck=16)

    stA = ExitStack()
    EA = stA.enter_context
    x_full = EA(nc.sbuf_tensor("x_full", [P, FC, TOK], F32))
    xprev = EA(nc.sbuf_tensor("xprev", [P, 4, XBB, D], BF16))
    xsq_t = EA(nc.sbuf_tensor("xsq_t", [P, 2, TTS], BF16))
    tmpA = EA(nc.sbuf_tensor("tmpA", [P, TTS], F32))
    wmc = EA(nc.sbuf_tensor("wmc", [NH, 1], F32))
    wvsA = EA(nc.sbuf_tensor("wvsA", [P, 2, D], BF16))
    lltA = EA(nc.sbuf_tensor("lltA", [1, NPB * NH], F32))
    row1b = EA(nc.sbuf_tensor("row1b", [1, NPB * NH], F32))
    bpx = EA(nc.sbuf_tensor("bpx", [1, NPB * NH], F32))
    t1s = EA(nc.sbuf_tensor("t1s", [P, NPB * NH], F32))
    E1t = EA(nc.sbuf_tensor("E1t", [P, NH], F32))
    er_col = EA(nc.sbuf_tensor("er_col", [P, 1], F32))
    er2 = EA(nc.sbuf_tensor("er2", [P, 1], F32))
    lamb_t = EA(nc.sbuf_tensor("lamb_t", [P, NH], F32))

    A = 0
    for t_sb, t_dr in ((s1_t, s1_in), (s2_t, s2_in), (sh1_t, sh1_in),
                       (sh2_t, sh2_in), (s3_t, s3_in), (gam_t, gam_in),
                       (bet_t, bet_in), (beta_sb, beta_r)):
        DMA(A, "sync", t_sb[:], t_dr[:], ld)
    for kc in range(FC):
        DMA(A, "sync", x_full[:, kc, :],
            bass.AP(x_in, kc * TOK, [[FC * TOK, P], [1, TOK]]), ldx)

    OP(A, "gpsimd", lambda e: e.iota(
        bpx[:], [[1, NPB], [0, NH]], channel_multiplier=0,
        allow_small_or_imprecise_dtypes=True))
    OP(A, "gpsimd", lambda e: e.iota(
        er_col[:], [[0, 1]], channel_multiplier=1,
        allow_small_or_imprecise_dtypes=True), inc=gio)
    for bi in range(NXB):
        if bi >= 4:
            W(A, "gpsimd", pewsum, bi - 3)
        par = bi % 4
        in_ap = bass.AP(xp_in, bi * XBB * P * D,
                        [[D, P], [P * D, XBB], [1, D]])
        out_ap = bass.AP(xprev, par * XBB * D,
                         [[4 * XBB * D, P], [D, XBB], [1, D]])
        DMA(A, "gpsimd", out_ap, in_ap, xp)

    u_r = rows[0:1, 0:16]
    lam_r = rows[0:1, 16:32]
    ll_r = rows[0:1, 32:48]
    w_r = rows[0:1, 48:64]
    lnw_r = rows[0:1, 64:80]
    m_r = rows[0:1, 80:96]

    llbP = PS[:, 3584:3600]
    E2P = PS[:, 3584:4096]

    OP(A, "vector", lambda e: e.memset(onesr[:], 1.0))
    OP(A, "vector", lambda e: e.memset(onesc_bf[:], 1.0), inc=vones)

    def lam_steps():
        def vy(fn):
            OP(A, "vector", fn, inc=vch)
            w = vch.n
            yield
            W(A, "vector", vch, w)
        W(A, "scalar", ld)
        OP(A, "scalar", lambda e: e.activation(
            out=u_r, in_=beta_sb[:], func=AF.Exp, scale=-1.0), inc=sst)
        yield
        W(A, "vector", sst, 1)
        yield from vy(lambda e: e.tensor_scalar_add(w_r, u_r, 1.0))
        OP(A, "vector", lambda e: e.reciprocal(out=lam_r, in_=w_r), inc=vpre)
        yield
        W(A, "vector", vpre, 1)
        W(A, "scalar", vpre, 1)
        OP(A, "scalar", lambda e: e.activation(
            out=lnw_r, in_=w_r, func=AF.Ln), inc=sst)
        W(A, "sync", vpre, 1)
        DMA(A, "sync", lamb_t[:],
            bass.AP(rows, 16, [[128, 1], [0, P], [1, NH]]), vlam)
        yield
        W(A, "vector", sst, 2)
        yield from vy(lambda e: e.tensor_scalar(
            ll_r, u_r, 1.0 / 7.0, -1.0 / 6.0, ALU.mult, ALU.add))
        for c in (1.0 / 5.0, -1.0 / 4.0, 1.0 / 3.0, -1.0 / 2.0, 1.0):
            yield from vy(lambda e: e.tensor_tensor(ll_r, ll_r, u_r, ALU.mult))
            yield from vy(lambda e, c=c: e.tensor_scalar_add(ll_r, ll_r, c))
        yield from vy(lambda e: e.tensor_tensor(ll_r, ll_r, u_r, ALU.mult))
        yield from vy(lambda e: e.tensor_single_scalar(
            m_r, u_r, 0.25, ALU.is_lt))
        yield from vy(lambda e: e.tensor_tensor(ll_r, ll_r, m_r, ALU.mult))
        yield from vy(lambda e: e.tensor_scalar(
            m_r, m_r, -1.0, 1.0, ALU.mult, ALU.add))
        yield from vy(lambda e: e.tensor_tensor(m_r, m_r, lnw_r, ALU.mult))
        yield from vy(lambda e: e.tensor_tensor(ll_r, ll_r, m_r, ALU.add))
        yield from vy(lambda e: e.tensor_scalar_mul(ll_r, ll_r, -1.0))
        OP(A, "vector", lambda e: e.tensor_scalar_max(ll_r, ll_r, -87.0),
           inc=vll)
        yield
        W(A, "vector", vll, 1)
        W(A, "tensor", vll, 1)
        W(A, "tensor", vones)
        OP(A, "tensor", lambda e: e.matmul(
            llbP[:], onesr[:], ll_r, start=True, stop=True), inc=peE2)
        W(A, "vector", gio)
        yield from vy(lambda e: e.tensor_scalar(
            er2[:], er_col[:], -1.0, 127.0, ALU.mult, ALU.add))
        W(A, "vector", peE2, 1)
        OP(A, "vector", lambda e: e.tensor_scalar_mul(
            E1t[:], llbP[:], er2[:, 0:1]), inc=vE1)
        W(A, "scalar", vE1, 1)
        OP(A, "scalar", lambda e: e.activation(
            out=E1t[:], in_=E1t[:], func=AF.Exp), inc=sE1)
        yield
        yield from vy(lambda e: e.tensor_scalar(
            row1b[:], bpx[:], -128.0, 6016.0, ALU.mult, ALU.add))
        yield from vy(lambda e: e.tensor_copy(
            out=bass.AP(lltA, 0, [[NPB * NH, 1], [NH, NPB], [1, NH]]),
            in_=bass.AP(rows, 32, [[128, 1], [0, NPB], [1, NH]])))
        OP(A, "vector", lambda e: e.tensor_tensor(
            row1b[:], row1b[:], lltA[:], ALU.mult), inc=vrb)
        W(A, "scalar", vrb, 1)
        OP(A, "scalar", lambda e: e.activation(
            out=row1b[:], in_=row1b[:], func=AF.Exp), inc=sE2)
        yield
        W(A, "tensor", sE2, 1)
        W(A, "tensor", vE1, 1)
        OP(A, "tensor", lambda e: e.matmul(
            E2P[:], onesr[:], row1b[0:1, 0:512], start=True, stop=True),
            inc=peE2)
        W(A, "vector", peE2, 2)
        W(A, "vector", sE1, 1)
        OP(A, "vector", lambda e: e.tensor_tensor(
            omg_pre[:, 0:512],
            bass.AP(E1t, 0, [[NH, P], [0, 32], [1, NH]]),
            E2P[:], ALU.mult), inc=somg)
        yield
        W(A, "tensor", somg, 1)
        OP(A, "tensor", lambda e: e.matmul(
            E2P[:, 0:256], onesr[:], row1b[0:1, 512:768],
            start=True, stop=True), inc=peE2)
        W(A, "vector", peE2, 3)
        OP(A, "vector", lambda e: e.tensor_tensor(
            omg_pre[:, 512:768],
            bass.AP(E1t, 0, [[NH, P], [0, 16], [1, NH]]),
            E2P[:, 0:256], ALU.mult), inc=somg)

    wsA = PS[0:NH, 0:512]
    wsB = PS[0:NH, 512:1024]
    wsMu = PS[0:NH, 1024:1025]
    A2 = PS[:, 1536:2048]
    B2 = PS[:, 2048:2560]
    SSp = PS[0:1, 2560:3584]

    sst_at = {}
    mm_em = [0]

    def prior_stats():
        for bi in range(NXB):
            while bi >= 4 and mm_em[0] < bi - 3:
                yield
            par = bi % 4

            def bap(s, c0, c1, p=par):
                return bass.AP(xprev, p * XBB * D + s * D + c0,
                               [[4 * XBB * D, P], [1, c1 - c0]])
            W(A, "vector", xp, 16 * (bi + 1))
            for s in range(XBB):
                OP(A, "vector", lambda e, s=s, f=bap: e.bn_stats(
                    out=bnAll[:, s, 0:6], in_=f(s, 0, 512)))
                OP(A, "vector", lambda e, s=s, f=bap: e.bn_stats(
                    out=bnAll[:, s, 6:12], in_=f(s, 512, 1024)),
                    inc=(vst if s == XBB - 1 else None))
                if s % 2 == 1:
                    yield
            W(A, "vector", vst)
            for s in range(XBB):
                OP(A, "vector", lambda e, s=s: e.bn_aggr(
                    out=mv6[:, 2 * s:2 * s + 2], in_=bnAll[:, s, 0:12]),
                    inc=(vst if s == XBB - 1 else None))
            yield
            W(A, "vector", vst)
            OP(A, "vector", lambda e, bi=bi: e.tensor_copy(
                out=mu_all[:, bi * XBB:(bi + 1) * XBB],
                in_=bass.AP(mv6, 0, [[2 * XBB, P], [2, XBB]])))
            OP(A, "vector", lambda e: e.tensor_scalar_add(
                var6[:], bass.AP(mv6, 1, [[2 * XBB, P], [2, XBB]]), EPS),
                inc=vst)
            yield
            W(A, "vector", vst)
            OP(A, "vector", lambda e, bi=bi: e.reciprocal(
                out=rstd_all[:, bi * XBB:(bi + 1) * XBB], in_=var6[:]),
                inc=vst)
            W(A, "scalar", vst)
            OP(A, "scalar", lambda e, bi=bi: e.activation(
                out=rstd_all[:, bi * XBB:(bi + 1) * XBB],
                in_=rstd_all[:, bi * XBB:(bi + 1) * XBB],
                func=AF.Sqrt), inc=sst)
            sst_at[bi] = sst.n
            yield

    def prior_mms():
        for bi in range(NXB):
            while bi not in sst_at:
                yield
            par = bi % 4

            def bap(s, c0, c1, p=par):
                return bass.AP(xprev, p * XBB * D + s * D + c0,
                               [[4 * XBB * D, P], [1, c1 - c0]])
            W(A, "vector", sst, sst_at[bi])
            W(A, "vector", somg, 2)
            if bi >= 4:
                W(A, "vector", pewsum, bi - 3)
            OP(A, "vector", lambda e, bi=bi, par=par: e.tensor_tensor(
                omg_b[:, bi % 4, :],
                omg_pre[:, bi * XBB * NH:(bi + 1) * XBB * NH],
                bass.AP(rstd_all, bi * XBB, [[NPB, P], [1, XBB], [0, NH]]),
                ALU.mult), inc=vomg)
            yield
            W(A, "tensor", vomg, bi + 1)
            W(A, "tensor", xp, 16 * (bi + 1))
            for s in range(XBB):
                bp = bi * XBB + s
                st_, sp_ = bp == 0, bp == NPB - 1
                OP(A, "tensor", lambda e, f=bap, s=s, bi=bi, st_=st_,
                   sp_=sp_: e.matmul(
                       wsA, omg_b[:, bi % 4, s * NH:(s + 1) * NH],
                       f(s, 0, 512), start=st_, stop=sp_))
                OP(A, "tensor", lambda e, f=bap, s=s, bi=bi, st_=st_,
                   sp_=sp_: e.matmul(
                       wsB, omg_b[:, bi % 4, s * NH:(s + 1) * NH],
                       f(s, 512, 1024), start=st_, stop=sp_))
                OP(A, "tensor", lambda e, bp=bp, s=s, bi=bi, st_=st_,
                   sp_=sp_: e.matmul(
                       wsMu, omg_b[:, bi % 4, s * NH:(s + 1) * NH],
                       mu_all[:, bp:bp + 1], start=st_, stop=sp_),
                   inc=(pewsum if s == XBB - 1 else None))
            mm_em[0] = bi + 1
            yield

    def phase1_steps():
        for kc in range(FC):
            W(A, "vector", ldx, 16 * (kc + 1))
            OP(A, "vector", lambda e, kc=kc: e.tensor_copy(
                out=xn[:, kc, :], in_=x_full[:, kc, :]), inc=vxb)
            if kc % 2 == 1:
                yield
        vxq0 = sxq.n
        peSS0 = peSS.n
        W(A, "tensor", vones)
        for tt in range(NTT):
            if tt >= 1:
                W(A, "tensor", vrow1, tt)
            for kc in range(FC):
                gi = tt * FC + kc
                if tt == 0:
                    W(A, "scalar", ldx, 16 * (kc + 1))
                    W(A, "tensor", vxb, kc + 1)
                if gi >= 2:
                    W(A, "scalar", peSS, peSS0 + gi - 1)
                OP(A, "scalar", lambda e, kc=kc, tt=tt, b=gi % 2: e.activation(
                    out=xsq_t[:, b, :],
                    in_=x_full[:, kc, tt * TTS:(tt + 1) * TTS],
                    func=AF.Square), inc=sxq)
                W(A, "tensor", sxq, vxq0 + gi + 1)
                OP(A, "tensor", lambda e, kc=kc, tt=tt: e.matmul(
                    SSp[:, 0:512], onesc_bf[:],
                    xn[:, kc, tt * TTS:(tt + 1) * TTS],
                    start=(kc == 0), stop=(kc == FC - 1)))
                OP(A, "tensor", lambda e, kc=kc, b=gi % 2: e.matmul(
                    SSp[:, 512:1024], onesc_bf[:], xsq_t[:, b, :],
                    start=(kc == 0), stop=(kc == FC - 1)), inc=peSS)
                if kc % 2 == 1:
                    yield
            W(A, "vector", peSS, peSS0 + (tt + 1) * FC)
            mu_s = mrow[0:1, (tt % 2) * TTS:(tt % 2 + 1) * TTS]
            a_s = arow[0:1, (tt % 2) * TTS:(tt % 2 + 1) * TTS]
            b_s = brow[0:1, (tt % 2) * TTS:(tt % 2 + 1) * TTS]
            OP(A, "vector", lambda e, m=mu_s: e.tensor_scalar_mul(
                m, SSp[:, 0:512], 1.0 / D))
            OP(A, "vector", lambda e, a=a_s: e.tensor_scalar_mul(
                a, SSp[:, 512:1024], 1.0 / D))
            OP(A, "vector", lambda e, m=mu_s: e.tensor_mul(tmpA[0:1, :], m, m))
            OP(A, "vector", lambda e, a=a_s: e.tensor_sub(a, a, tmpA[0:1, :]))
            OP(A, "vector", lambda e, a=a_s: e.tensor_scalar_add(a, a, EPS))
            OP(A, "vector", lambda e, a=a_s: e.reciprocal(out=a, in_=a),
               inc=vrow1)
            W(A, "scalar", vrow1)
            OP(A, "scalar", lambda e, a=a_s: e.activation(
                out=a, in_=a, func=AF.Sqrt), inc=srow)
            yield
            W(A, "vector", srow)
            OP(A, "vector", lambda e, m=mu_s, a=a_s, b=b_s:
               e.scalar_tensor_tensor(b, m, -1.0, a, ALU.mult, ALU.mult),
               inc=vrow2)
            W(A, "tensor", vrow2)
            if tt >= 1:
                W(A, "tensor", vxn, tt)
            OP(A, "tensor", lambda e, a=a_s: e.matmul(
                A2, onesr[:], a, start=True, stop=True))
            OP(A, "tensor", lambda e, b=b_s: e.matmul(
                B2, onesr[:], b, start=True, stop=True), inc=peAB)
            W(A, "vector", peAB)
            for kc in range(FC):
                OP(A, "vector", lambda e, kc=kc, tt=tt: e.tensor_mul(
                    tmpA[:], x_full[:, kc, tt * TTS:(tt + 1) * TTS], A2))
                OP(A, "vector", lambda e, kc=kc, tt=tt: e.tensor_add(
                    xn[:, kc, tt * TTS:(tt + 1) * TTS], tmpA[:], B2),
                    inc=(vxn if kc == FC - 1 else None))
                if kc % 2 == 1:
                    yield

    g0, g1a, g1b, g2 = (lam_steps(), prior_stats(), prior_mms(),
                        phase1_steps())
    done1a = done2 = False
    for _ in g0:
        if not done1a:
            done1a = next(g1a, "END") == "END"
        if not done2:
            done2 = next(g2, "END") == "END"
    done1b = False
    while not (done1a and done1b and done2):
        if not done1a:
            done1a = next(g1a, "END") == "END"
        if not done1b:
            done1b = next(g1b, "END") == "END"
        if not done2:
            done2 = next(g2, "END") == "END"

    W(A, "vector", pewsum)
    OP(A, "vector", lambda e: e.tensor_copy(out=wmc[:], in_=wsMu), inc=vwm)
    W(A, "vector", vwm)
    OP(A, "vector", lambda e: e.tensor_scalar(
        wadj_bf[:, 0:512], wsA, wmc[:, 0:1], None, ALU.subtract))
    OP(A, "vector", lambda e: e.tensor_scalar(
        wadj_bf[:, 512:1024], wsB, wmc[:, 0:1], None, ALU.subtract), inc=vwadj)
    W(A, "sync", vwadj)
    for kc in range(FC):
        OP(A, "sync", lambda e, kc=kc: e.dma_start_transpose(
            wadjT[:, kc * NH:(kc + 1) * NH], wadj_bf[:, kc * P:(kc + 1) * P]),
            inc=tps, inck=16)

    o3_0 = PS[:, 3584:3584 + NH]
    o3_1 = PS[:, 1536:1536 + NH]
    W(A, "tensor", somg, 2)
    gwv0 = gwv.n
    for mc in range(FC):
        if mc >= 2:
            W(A, "gpsimd", peo3, mc - 1)
        DMA(A, "gpsimd", wvsA[:, mc % 2, :],
            bass.AP(wv_in, mc * P * D, [[D, P], [1, D]]), gwv)
    W(A, "tensor", tps)
    W(A, "tensor", vxn)
    for mc in range(FC):
        o3 = o3_0 if mc % 2 == 0 else o3_1
        W(A, "tensor", gwv, gwv0 + 16 * (mc + 1))
        if mc >= 2:
            W(A, "tensor", vco3, mc - 1)
        for kc in range(FC):
            OP(A, "tensor", lambda e, o3=o3, kc=kc, mc=mc: e.matmul(
                o3, wvsA[:, mc % 2, kc * P:(kc + 1) * P],
                wadjT[:, kc * NH:(kc + 1) * NH],
                start=(kc == 0), stop=(kc == FC - 1)),
                inc=(peo3 if kc == FC - 1 else None))
        W(A, "vector", peo3)
        OP(A, "vector", lambda e, o3=o3, mc=mc: e.tensor_copy(
            out=carry_fm[0:64, mc:mc + 1], in_=o3[0:64, 2 * mc:2 * mc + 1]))
        OP(A, "vector", lambda e, o3=o3, mc=mc: e.tensor_copy(
            out=carry_fm[64:128, mc:mc + 1],
            in_=o3[64:128, 2 * mc + 1:2 * mc + 2]), inc=vco3)

    gwv_preB = gwv.n
    for mc in range(2):
        DMA(A, "gpsimd", wsl[:, mc, :],
            bass.AP(wv_in, mc * P * D, [[D, P], [1, D]]), gwv)

    for eng in ENGS:
        OP(A, eng, lambda e: e.nop(), inc=barA)

    with nc.Block() as blkA:
        PRG[0].run(blkA)
    stA.close()

    B = 1
    stB = ExitStack()
    EB = stB.enter_context
    state = EB(nc.sbuf_tensor("state", [P, FC, TOK], BF16))
    ybf = EB(nc.sbuf_tensor("ybf", [P, FC, TOK], BF16))
    ysqb = EB(nc.sbuf_tensor("ysqb", [P, 2, FC, TTS], BF16))
    a2y_sb = EB(nc.sbuf_tensor("a2y_sb", [P, TOK], F32))
    b2y_sb = EB(nc.sbuf_tensor("b2y_sb", [P, TOK], F32))
    gs_sb = EB(nc.sbuf_tensor("gs_sb", [P, FC, TOK], BF16))
    attev = EB(nc.sbuf_tensor("attev", [P, 2, TTS], F32))
    tmpB = EB(nc.sbuf_tensor("tmpB", [P, 2, TTS], F32))

    for eng in ENGS:
        W(B, eng, barA)

    W(B, "vector", vlam, 16)
    for mc in range(FC):
        OP(B, "vector", lambda e, mc=mc: e.tensor_copy(
            out=lam_fm[0:64, mc:mc + 1],
            in_=lamb_t[0:64, 2 * mc:2 * mc + 1]))
        OP(B, "vector", lambda e, mc=mc: e.tensor_copy(
            out=lam_fm[64:128, mc:mc + 1],
            in_=lamb_t[64:128, 2 * mc + 1:2 * mc + 2]))

    pb = [PS[:, i * 512:(i + 1) * 512] for i in range(8)]

    def attn_stream(w_dr, cur_sem, cur0, prev_sem, prev_val,
                    base=None, skip=0):
        if base is None:
            base = gwv.n
        for mc in range(skip, FC):
            if mc < 2:
                W(B, "gpsimd", prev_sem, prev_val)
            else:
                W(B, "gpsimd", cur_sem, cur0 + mc - 1)
            DMA(B, "gpsimd", wsl[:, mc % 2, :],
                bass.AP(w_dr, mc * P * D, [[D, P], [1, D]]), gwv)
        return base

    gb = attn_stream(wv_in, peV, peV.n, barA, 0, base=gwv_preB, skip=2)
    for mc in range(FC):
        par = mc % 2
        W(B, "tensor", gwv, gb + 16 * (mc + 1))
        if mc >= 2:
            W(B, "tensor", vsc, mc - 1)
        for kc in range(FC):
            for tt in range(NTT):
                OP(B, "tensor", lambda e, mc=mc, kc=kc, tt=tt, par=par: e.matmul(
                    pb[par * 4 + tt], wsl[:, par, kc * P:(kc + 1) * P],
                    xn[:, kc, tt * TTS:(tt + 1) * TTS],
                    start=(kc == 0), stop=(kc == FC - 1)),
                    inc=(peV if (kc == FC - 1 and tt == NTT - 1) else None))
        W(B, "vector", peV)
        OP(B, "vector", lambda e, mc=mc, par=par: e.tensor_tensor_scan(
            state[:, mc, :],
            bass.AP(lam_fm, mc, [[FC, P], [0, TOK]]),
            PS[:, par * 2048:(par + 1) * 2048],
            carry_fm[:, mc:mc + 1], ALU.mult, ALU.add), inc=vsc)

    peQ0, vy0, peV_tot = peQ.n, vy.n, peV.n
    gb = attn_stream(wq_in, peQ, peQ0, peV, peV_tot)
    W(B, "tensor", vsc)
    for mc in range(FC):
        par = mc % 2
        W(B, "tensor", gwv, gb + 16 * (mc + 1))
        if mc >= 2:
            W(B, "tensor", vy, vy0 + mc - 1)
        for kc in range(FC):
            for tt in range(NTT):
                OP(B, "tensor", lambda e, mc=mc, kc=kc, tt=tt, par=par: e.matmul(
                    pb[par * 4 + tt], wsl[:, par, kc * P:(kc + 1) * P],
                    xn[:, kc, tt * TTS:(tt + 1) * TTS],
                    start=(kc == 0), stop=(kc == FC - 1)),
                    inc=(peQ if (kc == FC - 1 and tt == NTT - 1) else None))
        W(B, "vector", peQ)
        for tt in range(NTT):
            OP(B, "vector", lambda e, mc=mc, tt=tt, par=par: e.tensor_mul(
                ybf[:, mc, tt * TTS:(tt + 1) * TTS], pb[par * 4 + tt],
                state[:, mc, tt * TTS:(tt + 1) * TTS]),
                inc=(vy if tt == NTT - 1 else None))

    peG0, sgs0 = peG.n, sgs.n
    peQ_tot, vy_tot = peQ.n, vy.n
    gb_g = attn_stream(wg_in, peG, peG0, peQ, peQ_tot)

    syP = PS[0:1, 2048:2560]
    sy2P = PS[0:1, 2560:3072]
    A2y = PS[:, 3072:3584]
    B2y = PS[:, 3584:4096]

    def gates_steps():
        W(B, "tensor", vy, vy_tot)
        for mc in range(FC):
            W(B, "tensor", gwv, gb_g + 16 * (mc + 1))
            if mc >= 1:
                W(B, "tensor", sgs, sgs0 + mc)
            for kc in range(FC):
                for tt in range(NTT):
                    OP(B, "tensor", lambda e, mc=mc, kc=kc, tt=tt: e.matmul(
                        pb[tt], wsl[:, mc % 2, kc * P:(kc + 1) * P],
                        xn[:, kc, tt * TTS:(tt + 1) * TTS],
                        start=(kc == 0), stop=(kc == FC - 1)),
                        inc=(peG if (kc == FC - 1 and tt == NTT - 1) else None))
                if kc % 2 == 1:
                    yield
            W(B, "scalar", peG, peG0 + mc + 1)
            OP(B, "scalar", lambda e, mc=mc: e.activation(
                out=gs_sb[:, mc, :], in_=PS[:, 0:2048], func=AF.Silu),
                inc=sgs)
            yield

    def stats_steps():
        vys0, peSSy0, vyr10, vab0, peABy0 = (vys.n, peSSy.n, vyr1.n, vab.n,
                                             peABy.n)
        for tt in range(NTT):
            par = tt % 2
            for mc in range(FC):
                OP(B, "vector", lambda e, mc=mc, tt=tt, par=par: e.tensor_mul(
                    ysqb[:, par, mc, :], ybf[:, mc, tt * TTS:(tt + 1) * TTS],
                    ybf[:, mc, tt * TTS:(tt + 1) * TTS]),
                    inc=(vys if mc == FC - 1 else None))
                if mc % 2 == 1:
                    yield
            W(B, "tensor", vys, vys0 + tt + 1)
            if tt >= 1:
                W(B, "tensor", vyr1, vyr10 + tt)
            for mc in range(FC):
                OP(B, "tensor", lambda e, mc=mc, tt=tt: e.matmul(
                    syP[:], onesc_bf[:], ybf[:, mc, tt * TTS:(tt + 1) * TTS],
                    start=(mc == 0), stop=(mc == FC - 1)))
                OP(B, "tensor", lambda e, mc=mc, par=par: e.matmul(
                    sy2P[:], onesc_bf[:], ysqb[:, par, mc, :],
                    start=(mc == 0), stop=(mc == FC - 1)),
                    inc=(peSSy if mc == FC - 1 else None))
                if mc % 2 == 1:
                    yield
            W(B, "vector", peSSy, peSSy0 + tt + 1)
            mu_s = mrow[0:1, par * TTS:(par + 1) * TTS]
            a_s = arow[0:1, par * TTS:(par + 1) * TTS]
            b_s = brow[0:1, par * TTS:(par + 1) * TTS]
            OP(B, "vector", lambda e, m=mu_s: e.tensor_scalar_mul(
                m, syP[:], 1.0 / D))
            OP(B, "vector", lambda e, a=a_s: e.tensor_scalar_mul(
                a, sy2P[:], 1.0 / D))
            OP(B, "vector", lambda e, m=mu_s: e.tensor_mul(tmpB[0:1, 0, :], m, m))
            OP(B, "vector", lambda e, a=a_s: e.tensor_sub(a, a, tmpB[0:1, 0, :]))
            OP(B, "vector", lambda e, a=a_s: e.tensor_scalar_add(a, a, EPS))
            OP(B, "vector", lambda e, a=a_s: e.reciprocal(out=a, in_=a),
               inc=vyr1)
            W(B, "scalar", vyr1)
            OP(B, "scalar", lambda e, a=a_s: e.activation(
                out=a, in_=a, func=AF.Sqrt), inc=syr)
            yield
            W(B, "vector", syr)
            OP(B, "vector", lambda e, m=mu_s, a=a_s, b=b_s:
               e.scalar_tensor_tensor(b, m, -1.0, a, ALU.mult, ALU.mult),
               inc=vyr2)
            W(B, "tensor", vyr2, tt + 1)
            if tt >= 1:
                W(B, "tensor", vab, vab0 + tt)
            OP(B, "tensor", lambda e, a=a_s: e.matmul(
                A2y, onesr[:], a, start=True, stop=True))
            OP(B, "tensor", lambda e, b=b_s: e.matmul(
                B2y, onesr[:], b, start=True, stop=True), inc=peABy)
            W(B, "vector", peABy, peABy0 + tt + 1)
            OP(B, "vector", lambda e, tt=tt: e.tensor_copy(
                out=a2y_sb[:, tt * TTS:(tt + 1) * TTS], in_=A2y))
            OP(B, "vector", lambda e, tt=tt: e.tensor_copy(
                out=b2y_sb[:, tt * TTS:(tt + 1) * TTS], in_=B2y), inc=vab)
            yield

    gg, gs2 = gates_steps(), stats_steps()
    dg = ds = False
    while not (dg and ds):
        if not dg:
            dg = next(gg, "END") == "END"
        if not ds:
            ds = next(gs2, "END") == "END"

    peWo0, vy20, peG_tot = peWo.n, vy2.n, peG.n
    vab_tot, sgs_tot0 = vab.n, sgs0
    gb_o = attn_stream(wo_in, peWo, peWo0, peG, peG_tot)

    def y2_steps():
        W(B, "vector", vab, vab_tot)
        for mc in range(FC):
            W(B, "vector", sgs, sgs0 + mc + 1)
            for tt in range(NTT):
                OP(B, "vector", lambda e, mc=mc, tt=tt: e.tensor_mul(
                    tmpB[:, 0, :], ybf[:, mc, tt * TTS:(tt + 1) * TTS],
                    a2y_sb[:, tt * TTS:(tt + 1) * TTS]))
                OP(B, "vector", lambda e, mc=mc, tt=tt: e.tensor_add(
                    tmpB[:, 1, :], tmpB[:, 0, :],
                    b2y_sb[:, tt * TTS:(tt + 1) * TTS]))
                OP(B, "vector", lambda e, mc=mc: e.tensor_scalar(
                    tmpB[:, 0, :], tmpB[:, 1, :], gam_t[:, mc:mc + 1],
                    bet_t[:, mc:mc + 1], ALU.mult, ALU.add))
                OP(B, "vector", lambda e, mc=mc, tt=tt: e.tensor_mul(
                    ybf[:, mc, tt * TTS:(tt + 1) * TTS], tmpB[:, 0, :],
                    gs_sb[:, mc, tt * TTS:(tt + 1) * TTS]),
                    inc=(vy2 if tt == NTT - 1 else None))
            yield

    def wo_steps():
        sp_i = 0
        vy2_seen = 0
        W(B, "tensor", sgs, sgs0 + FC)
        for mc in range(FC):
            W(B, "tensor", gwv, gb_o + 16 * (mc + 1))
            if mc >= 2:
                W(B, "tensor", sWo, (mc - 1) * NTT)
            for kc in range(FC):
                nonlocal_need = kc + 1
                if nonlocal_need > vy2_seen:
                    W(B, "tensor", vy2, vy20 + nonlocal_need)
                    vy2_seen = nonlocal_need
                for tt in range(NTT):
                    OP(B, "tensor", lambda e, mc=mc, kc=kc, tt=tt: e.matmul(
                        pb[(mc % 2) * 4 + tt],
                        wsl[:, mc % 2, kc * P:(kc + 1) * P],
                        ybf[:, kc, tt * TTS:(tt + 1) * TTS],
                        start=(kc == 0), stop=(kc == FC - 1)),
                        inc=(peWo if (kc == FC - 1 and tt == NTT - 1)
                             else None))
                if kc % 2 == 1:
                    yield
            W(B, "scalar", peWo)
            for tt in range(NTT):
                par2 = sp_i % 2
                if sp_i >= 2:
                    W(B, "scalar", spl, 16 * (sp_i - 1))
                OP(B, "scalar", lambda e, tt=tt, mc=mc, par2=par2: e.copy(
                    attev[:, par2, :], pb[(mc % 2) * 4 + tt]), inc=sWo)
                W(B, "sync", sWo)
                DMA(B, "sync",
                    bass.AP(attn_sp, mc * TOK + tt * TTS,
                            [[FC * TOK, P], [1, TTS]]),
                    attev[:, par2, :], spl)
                sp_i += 1
            yield

    gy, gw = y2_steps(), wo_steps()
    dy = dw = False
    while not (dy and dw):
        if not dy:
            dy = next(gy, "END") == "END"
        if not dw:
            dw = next(gw, "END") == "END"

    gw12_pre = gw12.n
    for mc in range(2):
        DMA(B, "gpsimd", w12b[:, mc, :],
            bass.AP(w12_in, mc * P * 2048, [[2048, P], [1, 2048]]), gw12)

    for eng in ENGS:
        OP(B, eng, lambda e: e.nop(), inc=barB)

    with nc.Block() as blkB:
        PRG[1].run(blkB)
    stB.close()
    mrow_cm.__exit__(None, None, None)
    brow_cm.__exit__(None, None, None)
    arow_cm.__exit__(None, None, None)

    C = 2
    stC = ExitStack()
    EC = stC.enter_context
    u_t = EC(nc.sbuf_tensor("u_t", [P, HC, 1024], BF16))
    h_t = EC(nc.sbuf_tensor("h_t", [P, HC, 1024], BF16))
    hbuf = EC(nc.sbuf_tensor("hbuf", [P, 3, 2048], BF16))
    u1sb = EC(nc.sbuf_tensor("u1sb", [P, 2, 1024], BF16))
    att_t = EC(nc.sbuf_tensor("att_t", [P, 2, TTS], F32))
    x_t = EC(nc.sbuf_tensor("x_t", [P, 2, TTS], F32))
    oB = EC(nc.sbuf_tensor("oB", [P, 2, TTS], F32))

    for eng in ENGS:
        W(C, eng, barB)
    W(C, "sync", spl)

    u1ps = [PS[:, 0:1024], PS[:, 1024:2048]]
    u2ps = [PS[:, 2048:3072], PS[:, 3072:4096]]
    hps = u1ps
    w3ps = u2ps

    jg_cell = [0]

    def emit_hloads(w_dr, pairs):
        for mc, hf in pairs:
            jg = jg_cell[0]
            jg_cell[0] += 1
            if jg >= 3:
                W(C, "gpsimd", peFHh, jg - 2)
            DMA(C, "gpsimd", hbuf[:, jg % 3, :],
                bass.AP(w_dr, mc * P * DH + hf * 2048,
                        [[DH, P], [1, 2048]]), gh)

    for p in range(2):
        TO = p * 1024

        vU0, sU0, peFU0 = vU.n, sU.n, peFU.n
        gw0 = gw12_pre if p == 0 else gw12.n
        h1_st = None
        W(C, "gpsimd", peFU)
        W(C, "tensor", vO)
        for mc in range(HC):
            par = mc % 2
            if mc == 2:
                h1_st = (jg_cell[0], gh.n)
                emit_hloads(h1_in, [(0, 0), (0, 1)])
            if not (p == 0 and mc < 2):
                if mc >= 2:
                    W(C, "gpsimd", peFU, peFU0 + mc - 1)
                DMA(C, "gpsimd", w12b[:, par, :],
                    bass.AP(w12_in, mc * P * 2048, [[2048, P], [1, 2048]]),
                    gw12)
            W(C, "tensor", gw12, gw0 + 16 * (mc + 1))
            if mc >= 2:
                W(C, "tensor", sU, sU0 + mc - 1)
                W(C, "tensor", vU, vU0 + mc - 1)
            for kc in range(FC):
                for ng in range(2):
                    OP(C, "tensor", lambda e, kc=kc, ng=ng, par=par, TO=TO:
                       e.matmul(
                           u1ps[par][:, ng * 512:(ng + 1) * 512],
                           w12b[:, par, kc * P:(kc + 1) * P],
                           xn[:, kc, TO + ng * 512:TO + (ng + 1) * 512],
                           start=(kc == 0), stop=(kc == FC - 1)))
                    OP(C, "tensor", lambda e, kc=kc, ng=ng, par=par, TO=TO:
                       e.matmul(
                           u2ps[par][:, ng * 512:(ng + 1) * 512],
                           w12b[:, par, D + kc * P: D + (kc + 1) * P],
                           xn[:, kc, TO + ng * 512:TO + (ng + 1) * 512],
                           start=(kc == 0), stop=(kc == FC - 1)),
                        inc=(peFU if (kc == FC - 1 and ng == 1) else None))
            W(C, "scalar", peFU)
            if mc >= 2:
                W(C, "scalar", vU, vU0 + mc - 1)
            OP(C, "scalar", lambda e, mc=mc, par=par: e.activation(
                out=u1sb[:, par, :], in_=u1ps[par][:], func=AF.Silu,
                scale=s1_t[:, mc:mc + 1]), inc=sU)
            W(C, "vector", sU)
            OP(C, "vector", lambda e, mc=mc, par=par: e.scalar_tensor_tensor(
                u_t[:, mc, :], u2ps[par][:], s2_t[:, mc:mc + 1], u1sb[:, par, :],
                ALU.mult, ALU.mult), inc=vU)

        def ffn_wide(w_dr, src_t, dst_t, scale_t, st=None):
            sH0 = sH.n
            if st is None:
                st = (jg_cell[0], gh.n)
                n_pre = 0
            else:
                n_pre = 2
            jg0, gh0 = st
            for mc in range(HC):
                for hf in range(2):
                    j = 2 * mc + hf
                    if j >= n_pre:
                        emit_hloads(w_dr, [(mc, hf)])
                par = mc % 2
                if mc >= 2:
                    W(C, "tensor", sH, sH0 + mc - 1)
                for kc in range(HC):
                    j = 2 * mc + kc // 16
                    if kc % 16 == 0:
                        W(C, "tensor", gh, gh0 + 16 * (j + 1))
                    for ng in range(2):
                        OP(C, "tensor",
                           lambda e, kc=kc, ng=ng, par=par, sl=(jg0 + j) % 3,
                           s=src_t: e.matmul(
                               hps[par][:, ng * 512:(ng + 1) * 512],
                               hbuf[:, sl, (kc % 16) * P:((kc % 16) + 1) * P],
                               s[:, kc, ng * 512:(ng + 1) * 512],
                               start=(kc == 0), stop=(kc == HC - 1)),
                           inc=(peFHh if (kc % 16 == 15 and ng == 1) else None))
                W(C, "scalar", peFHh)
                OP(C, "scalar", lambda e, mc=mc, par=par, d=dst_t, sc=scale_t:
                   e.activation(out=d[:, mc, :], in_=hps[par][:], func=AF.Gelu,
                                scale=sc[:, mc:mc + 1]), inc=sH)

        W(C, "tensor", vU)
        ffn_wide(h1_in, u_t, h_t, sh1_t, st=h1_st)
        W(C, "tensor", sH)
        ffn_wide(h2_in, h_t, u_t, sh2_t)

        vO0, sax0, sout0 = vO.n, sax.n, sout.n
        W(C, "tensor", sH)
        jg0w, gh0w = jg_cell[0], gh.n
        for mc in range(FC):
            emit_hloads(w3_in, [(mc, 0), (mc, 1)])
            par = mc % 2
            if mc >= 1:
                W(C, "sync", vO, vO0 + 2 * mc)
            for ng in range(2):
                DMA(C, "sync", att_t[:, ng, :],
                    bass.AP(attn_sp, mc * TOK + TO + ng * TTS,
                            [[FC * TOK, P], [1, TTS]]), sax)
                DMA(C, "sync", x_t[:, ng, :],
                    bass.AP(x_in, mc * TOK + TO + ng * TTS,
                            [[FC * TOK, P], [1, TTS]]), sax)
            if mc >= 2:
                W(C, "tensor", vO, vO0 + 2 * (mc - 1))
            for kc in range(HC):
                j = 2 * mc + kc // 16
                if kc % 16 == 0:
                    W(C, "tensor", gh, gh0w + 16 * (j + 1))
                for ng in range(2):
                    OP(C, "tensor", lambda e, kc=kc, ng=ng, par=par,
                       sl=(jg0w + j) % 3: e.matmul(
                           w3ps[par][:, ng * 512:(ng + 1) * 512],
                           hbuf[:, sl, (kc % 16) * P:((kc % 16) + 1) * P],
                           u_t[:, kc, ng * 512:(ng + 1) * 512],
                           start=(kc == 0), stop=(kc == HC - 1)),
                        inc=(peFHh if (kc % 16 == 15 and ng == 1) else None))
            W(C, "vector", peFHh)
            W(C, "vector", sax, sax0 + 64 * (mc + 1))
            if mc >= 1:
                W(C, "vector", sout, sout0 + 32 * mc)
            for ng in range(2):
                OP(C, "vector", lambda e, mc=mc, ng=ng, par=par:
                   e.scalar_tensor_tensor(
                       oB[:, ng, :], w3ps[par][:, ng * 512:(ng + 1) * 512],
                       s3_t[:, mc:mc + 1], att_t[:, ng, :], ALU.mult, ALU.add))
                OP(C, "vector", lambda e, ng=ng: e.tensor_add(
                    oB[:, ng, :], oB[:, ng, :], x_t[:, ng, :]), inc=vO)
                W(C, "sync", vO)
                DMA(C, "sync",
                    bass.AP(out_fm, mc * TOK + TO + ng * TTS,
                            [[FC * TOK, P], [1, TTS]]),
                    oB[:, ng, :], sout)

    W(C, "sync", sout)

    with nc.Block() as blkC:
        PRG[2].run(blkC)
    stC.close()
    xn_cm.__exit__(None, None, None)
    stC.close()
    ctx.close()
    nc.finalize()
    return nc


_NC_CACHE = None


def _get_nc():
    global _NC_CACHE
    if _NC_CACHE is None:
        _NC_CACHE = build_program()
    return _NC_CACHE


def _prep_weights(L0_attn_Wq, L0_attn_Wv, L0_attn_Wo, L0_attn_Wg,
                  L0_attn_beta, L0_attn_ln_gamma, L0_attn_ln_beta,
                  L0_ffn_w1_w, L0_ffn_w1_s, L0_ffn_w2_w, L0_ffn_w2_s,
                  L0_ffn_h1_w, L0_ffn_h1_s, L0_ffn_h2_w, L0_ffn_h2_s,
                  L0_ffn_w3_w, L0_ffn_w3_s):
    f32 = np.float32
    c = np.ascontiguousarray

    def tile_lhsT(WT, nmc, nkc):
        t = WT.reshape(nkc, P, nmc, P).transpose(2, 1, 0, 3)
        return c(t.reshape(nmc, P, nkc * P)).astype(f32)

    d = {}
    d["wq_in"] = tile_lhsT(np.asarray(L0_attn_Wq, f32).T, FC, FC)
    d["wv_in"] = tile_lhsT(np.asarray(L0_attn_Wv, f32).T, FC, FC)
    d["wg_in"] = tile_lhsT(np.asarray(L0_attn_Wg, f32).T, FC, FC)
    d["wo_in"] = tile_lhsT(np.asarray(L0_attn_Wo, f32).T, FC, FC)
    w1t = tile_lhsT(np.asarray(L0_ffn_w1_w, f32).T, HC, FC)
    w2t = tile_lhsT(np.asarray(L0_ffn_w2_w, f32).T, HC, FC)
    d["w12_in"] = c(np.concatenate([w1t, w2t], axis=2))
    d["h1_in"] = tile_lhsT(np.asarray(L0_ffn_h1_w, f32).T, HC, HC)
    d["h2_in"] = tile_lhsT(np.asarray(L0_ffn_h2_w, f32).T, HC, HC)
    d["w3_in"] = tile_lhsT(np.asarray(L0_ffn_w3_w, f32).T, FC, HC)
    d["s1_in"] = c(np.asarray(L0_ffn_w1_s, f32).reshape(HC, P).T)
    d["s2_in"] = c(np.asarray(L0_ffn_w2_s, f32).reshape(HC, P).T)
    d["sh1_in"] = c(np.asarray(L0_ffn_h1_s, f32).reshape(HC, P).T)
    d["sh2_in"] = c(np.asarray(L0_ffn_h2_s, f32).reshape(HC, P).T)
    d["s3_in"] = c(np.asarray(L0_ffn_w3_s, f32).reshape(FC, P).T)
    d["gam_in"] = c(np.asarray(L0_attn_ln_gamma, f32).reshape(FC, P).T)
    d["bet_in"] = c(np.asarray(L0_attn_ln_beta, f32).reshape(FC, P).T)
    d["beta_r"] = c(np.asarray(L0_attn_beta, f32).reshape(1, NH))
    return d


def kernel(L0_attn_x, L0_attn_Wq, L0_attn_Wv, L0_attn_Wo, L0_attn_Wg,
           L0_attn_beta, L0_attn_ln_gamma, L0_attn_ln_beta,
           L0_ffn_w1_w, L0_ffn_w1_s, L0_ffn_w2_w, L0_ffn_w2_s,
           L0_ffn_h1_w, L0_ffn_h1_s, L0_ffn_h2_w, L0_ffn_h2_s,
           L0_ffn_w3_w, L0_ffn_w3_s):
    x = np.asarray(L0_attn_x, np.float32)
    nc_ = _get_nc()
    wd = _prep_weights(L0_attn_Wq, L0_attn_Wv, L0_attn_Wo, L0_attn_Wg,
                       L0_attn_beta, L0_attn_ln_gamma, L0_attn_ln_beta,
                       L0_ffn_w1_w, L0_ffn_w1_s, L0_ffn_w2_w, L0_ffn_w2_s,
                       L0_ffn_h1_w, L0_ffn_h1_s, L0_ffn_h2_w, L0_ffn_h2_s,
                       L0_ffn_w3_w, L0_ffn_w3_s)

    in_maps = []
    for c_id in range(8):
        b, q = c_id // 4, c_id % 4
        xc = x[b, q * TOK:(q + 1) * TOK, :]
        x_fm = np.ascontiguousarray(
            xc.T.reshape(FC, P, TOK).transpose(1, 0, 2).reshape(P, FC * TOK))
        lo = q * TOK - PRI
        xpv = np.zeros((PRI, D), np.float32)
        if q > 0:
            xpv[max(0, -lo):] = x[b, max(0, lo):q * TOK, :]
        m = dict(wd)
        m["x_in"] = x_fm
        m["xp_in"] = xpv
        in_maps.append(m)

    res = run_bass_kernel_spmd(nc_, in_maps, core_ids=list(range(8)))

    out = np.empty_like(x)
    for c_id in range(8):
        b, q = c_id // 4, c_id % 4
        o = res.results[c_id]["out_fm"]
        out[b, q * TOK:(q + 1) * TOK, :] = \
            o.reshape(P, FC, TOK).transpose(1, 0, 2).reshape(D, TOK).T
    return out


# version 15
# speedup vs baseline: 1.0657x; 1.0657x over previous
import numpy as np
import ml_dtypes
from contextlib import ExitStack

import concourse.bass as bass
import concourse.mybir as mybir
from concourse.bass_utils import run_bass_kernel_spmd

F32 = mybir.dt.float32
BF16 = mybir.dt.bfloat16
AF = mybir.ActivationFunctionType
ALU = mybir.AluOpType

P = 128
TOK = 2048
NTT = 4
TTS = 512
FC = 8
HC = 32
D = 1024
DH = 4096
NH = 16
PRI = 6144
XBB = 6
NXB = 8
NPB = 48
EPS = 1e-5

ENGS = ("tensor", "vector", "scalar", "gpsimd", "sync")


class Prog:
    def __init__(self):
        self.q = {e: [] for e in ENGS}

    def emit(self, eng, fn):
        self.q[eng].append(fn)

    def run(self, block):
        for eng in ENGS:
            fns = self.q[eng]
            if not fns:
                continue
            getattr(block, eng)(lambda e, fns=fns: [fn(e) for fn in fns])


class Sem:
    def __init__(self, stack, nc, name):
        self.h = stack.enter_context(nc.semaphore(name))
        self.n = 0


def build_program():
    nc = bass.Bass("TRN2", target_bir_lowering=False, detect_race_conditions=False)

    x_in = nc.dram_tensor("x_in", [P, FC * TOK], F32, kind="ExternalInput")
    xp_in = nc.dram_tensor("xp_in", [PRI, D], F32, kind="ExternalInput")
    beta_r = nc.dram_tensor("beta_r", [1, NH], F32, kind="ExternalInput")
    wq_in = nc.dram_tensor("wq_in", [FC, P, D], F32, kind="ExternalInput")
    wv_in = nc.dram_tensor("wv_in", [FC, P, D], F32, kind="ExternalInput")
    wg_in = nc.dram_tensor("wg_in", [FC, P, D], F32, kind="ExternalInput")
    wo_in = nc.dram_tensor("wo_in", [FC, P, D], F32, kind="ExternalInput")
    gam_in = nc.dram_tensor("gam_in", [P, FC], F32, kind="ExternalInput")
    bet_in = nc.dram_tensor("bet_in", [P, FC], F32, kind="ExternalInput")
    w12_in = nc.dram_tensor("w12_in", [HC, P, 2048], F32, kind="ExternalInput")
    h1_in = nc.dram_tensor("h1_in", [HC, P, DH], F32, kind="ExternalInput")
    h2_in = nc.dram_tensor("h2_in", [HC, P, DH], F32, kind="ExternalInput")
    w3_in = nc.dram_tensor("w3_in", [FC, P, DH], F32, kind="ExternalInput")
    s1_in = nc.dram_tensor("s1_in", [P, HC], F32, kind="ExternalInput")
    s2_in = nc.dram_tensor("s2_in", [P, HC], F32, kind="ExternalInput")
    sh1_in = nc.dram_tensor("sh1_in", [P, HC], F32, kind="ExternalInput")
    sh2_in = nc.dram_tensor("sh2_in", [P, HC], F32, kind="ExternalInput")
    s3_in = nc.dram_tensor("s3_in", [P, FC], F32, kind="ExternalInput")

    out_fm = nc.dram_tensor("out_fm", [P, FC * TOK], F32, kind="ExternalOutput")
    attn_sp = nc.dram_tensor("attn_sp", [P, FC * TOK], F32, kind="Internal")
    xn_sp = nc.dram_tensor("xn_sp", [P, FC * TOK], BF16, kind="Internal")

    ctx = ExitStack()
    E = ctx.enter_context

    s1_t = E(nc.sbuf_tensor("s1_t", [P, HC], F32))
    s2_t = E(nc.sbuf_tensor("s2_t", [P, HC], F32))
    sh1_t = E(nc.sbuf_tensor("sh1_t", [P, HC], F32))
    sh2_t = E(nc.sbuf_tensor("sh2_t", [P, HC], F32))
    s3_t = E(nc.sbuf_tensor("s3_t", [P, FC], F32))
    gam_t = E(nc.sbuf_tensor("gam_t", [P, FC], F32))
    bet_t = E(nc.sbuf_tensor("bet_t", [P, FC], F32))
    beta_sb = E(nc.sbuf_tensor("beta_sb", [1, NH], F32))
    rows = E(nc.sbuf_tensor("rows", [1, 128], F32))
    erow = E(nc.sbuf_tensor("erow", [1, P], F32))
    onesr = E(nc.sbuf_tensor("onesr", [1, P], F32))
    onesc_bf = E(nc.sbuf_tensor("onesc_bf", [P, 1], BF16))
    omg_pre = E(nc.sbuf_tensor("omg_pre", [P, NPB * NH], BF16))
    omg_b = E(nc.sbuf_tensor("omg_b", [P, 4, XBB * NH], BF16))
    mu_all = E(nc.sbuf_tensor("mu_all", [P, NPB], BF16))
    rstd_all = E(nc.sbuf_tensor("rstd_all", [P, NPB], F32))
    bnAll = E(nc.sbuf_tensor("bnAll", [P, XBB, 12], F32))
    mv6 = E(nc.sbuf_tensor("mv6", [P, 2 * XBB], F32))
    var6 = E(nc.sbuf_tensor("var6", [P, XBB], F32))
    wadj_bf = E(nc.sbuf_tensor("wadj_bf", [NH, D], BF16))
    wadjT = E(nc.sbuf_tensor("wadjT", [P, FC * NH], BF16))
    carry_fm = E(nc.sbuf_tensor("carry_fm", [P, FC], F32))
    lam_fm = E(nc.sbuf_tensor("lam_fm", [P, FC], F32))
    w12b = E(nc.sbuf_tensor("w12b", [P, 2, 2048], BF16))
    wsl = E(nc.sbuf_tensor("wsl", [P, 2, D], BF16))
    PS = E(nc.psum_tensor("PS", [P, 4096], F32))

    xn_cm = nc.sbuf_tensor("xn", [P, FC, TOK], BF16)
    xn = xn_cm.__enter__()
    arow_cm = nc.sbuf_tensor("arow", [1, 1024], F32)
    arow = arow_cm.__enter__()
    brow_cm = nc.sbuf_tensor("brow", [1, 1024], F32)
    brow = brow_cm.__enter__()
    mrow_cm = nc.sbuf_tensor("mrow", [1, 1024], F32)
    mrow = mrow_cm.__enter__()

    sstk = ExitStack()
    ctx.enter_context(sstk)

    def mks(name):
        return Sem(sstk, nc, name)

    ld = mks("ld")
    ldx = mks("ldx")
    gwv = mks("gwv")
    xp = mks("xp")
    vpre = mks("vpre")
    gio = mks("gio")
    peT1 = mks("peT1")
    somg = mks("somg")
    vlam = mks("vlam")
    vst = mks("vst")
    sst = mks("sst")
    vomg = mks("vomg")
    pewsum = mks("pewsum")
    vwadj = mks("vwadj")
    vwm = mks("vwm")
    vch = mks("vch")
    tps = mks("tps")
    peo3 = mks("peo3")
    vco3 = mks("vco3")
    vxb = mks("vxb")
    vxq = mks("vxq")
    sxq = mks("sxq")
    peE2 = mks("peE2")
    peSum = mks("peSum")
    vev = mks("vev")
    sAB = mks("sAB")
    vll = mks("vll")
    vrb = mks("vrb")
    vE1 = mks("vE1")
    sE1 = mks("sE1")
    sE2 = mks("sE2")
    gllb = mks("gllb")
    gE2 = mks("gE2")
    vones = mks("vones")
    peSS = mks("peSS")
    vrow1 = mks("vrow1")
    srow = mks("srow")
    vrow2 = mks("vrow2")
    peAB = mks("peAB")
    vxn = mks("vxn")
    barA = mks("barA")
    peV = mks("peV")
    vsc = mks("vsc")
    peQ = mks("peQ")
    vy = mks("vy")
    vys = mks("vys")
    peSSy = mks("peSSy")
    vyr1 = mks("vyr1")
    syr = mks("syr")
    vyr2 = mks("vyr2")
    peABy = mks("peABy")
    vab = mks("vab")
    peG = mks("peG")
    sgs = mks("sgs")
    vy2 = mks("vy2")
    v2s = mks("v2s")
    s2v = mks("s2v")
    peWo = mks("peWo")
    sWo = mks("sWo")
    spl = mks("spl")
    sxn = mks("sxn")
    barB = mks("barB")
    gxn = mks("gxn")
    gw12 = mks("gw12")
    peFU = mks("peFU")
    sU = mks("sU")
    vU = mks("vU")
    gh = mks("gh")
    peFHh = mks("peFHh")
    sH = mks("sH")
    sax = mks("sax")
    vO = mks("vO")
    sout = mks("sout")

    PRG = [Prog(), Prog(), Prog()]

    def W(blk, eng, sem, val=None):
        v = sem.n if val is None else val
        if v <= 0:
            return
        PRG[blk].emit(eng, lambda e, s=sem.h, v=v: e.wait_ge(s, v))

    def OP(blk, eng, fn, inc=None, inck=1):
        if inc is not None:
            inc.n += inck
        def run(e, fn=fn, inc=inc, inck=inck):
            inst = fn(e)
            if inc is not None:
                inst.then_inc(inc.h, inck)
            return inst
        PRG[blk].emit(eng, run)

    def VOP(blk, fn):
        OP(blk, "vector", fn, inc=vch)
        W(blk, "vector", vch)

    def DMA(blk, eng, out_ap, in_ap, inc):
        OP(blk, eng, lambda e, o=out_ap, i=in_ap: e.dma_start(out=o, in_=i),
           inc=inc, inck=16)

    stA = ExitStack()
    EA = stA.enter_context
    x_full = EA(nc.sbuf_tensor("x_full", [P, FC, TOK], F32))
    xprev = EA(nc.sbuf_tensor("xprev", [P, 4, XBB, D], BF16))
    xsq_t = EA(nc.sbuf_tensor("xsq_t", [P, 2, TTS], BF16))
    tmpA = EA(nc.sbuf_tensor("tmpA", [P, TTS], F32))
    wmc = EA(nc.sbuf_tensor("wmc", [NH, 1], F32))
    wvsA = EA(nc.sbuf_tensor("wvsA", [P, 2, D], BF16))
    lltA = EA(nc.sbuf_tensor("lltA", [1, NPB * NH], F32))
    row1b = EA(nc.sbuf_tensor("row1b", [1, NPB * NH], F32))
    bpx = EA(nc.sbuf_tensor("bpx", [1, NPB * NH], F32))
    t1s = EA(nc.sbuf_tensor("t1s", [P, NPB * NH], F32))
    E1t = EA(nc.sbuf_tensor("E1t", [P, NH], F32))
    er_col = EA(nc.sbuf_tensor("er_col", [P, 1], F32))
    er2 = EA(nc.sbuf_tensor("er2", [P, 1], F32))
    lamb_t = EA(nc.sbuf_tensor("lamb_t", [P, NH], F32))

    A = 0
    for t_sb, t_dr in ((s1_t, s1_in), (s2_t, s2_in), (sh1_t, sh1_in),
                       (sh2_t, sh2_in), (s3_t, s3_in), (gam_t, gam_in),
                       (bet_t, bet_in), (beta_sb, beta_r)):
        DMA(A, "sync", t_sb[:], t_dr[:], ld)
    for kc in range(FC):
        DMA(A, "sync", x_full[:, kc, :],
            bass.AP(x_in, kc * TOK, [[FC * TOK, P], [1, TOK]]), ldx)

    OP(A, "gpsimd", lambda e: e.iota(
        bpx[:], [[1, NPB], [0, NH]], channel_multiplier=0,
        allow_small_or_imprecise_dtypes=True))
    OP(A, "gpsimd", lambda e: e.iota(
        er_col[:], [[0, 1]], channel_multiplier=1,
        allow_small_or_imprecise_dtypes=True), inc=gio)
    for bi in range(NXB):
        if bi >= 4:
            W(A, "gpsimd", pewsum, bi - 3)
        par = bi % 4
        in_ap = bass.AP(xp_in, bi * XBB * P * D,
                        [[D, P], [P * D, XBB], [1, D]])
        out_ap = bass.AP(xprev, par * XBB * D,
                         [[4 * XBB * D, P], [D, XBB], [1, D]])
        DMA(A, "gpsimd", out_ap, in_ap, xp)

    u_r = rows[0:1, 0:16]
    lam_r = rows[0:1, 16:32]
    ll_r = rows[0:1, 32:48]
    w_r = rows[0:1, 48:64]
    lnw_r = rows[0:1, 64:80]
    m_r = rows[0:1, 80:96]

    llbP = PS[:, 3584:3600]
    E2P = PS[:, 3584:4096]

    OP(A, "vector", lambda e: e.memset(onesr[:], 1.0))
    OP(A, "vector", lambda e: e.memset(onesc_bf[:], 1.0), inc=vones)

    def lam_steps():
        def vy(fn):
            OP(A, "vector", fn, inc=vch)
            w = vch.n
            yield
            W(A, "vector", vch, w)
        W(A, "scalar", ld)
        OP(A, "scalar", lambda e: e.activation(
            out=u_r, in_=beta_sb[:], func=AF.Exp, scale=-1.0), inc=sst)
        yield
        W(A, "vector", sst, 1)
        yield from vy(lambda e: e.tensor_scalar_add(w_r, u_r, 1.0))
        OP(A, "vector", lambda e: e.reciprocal(out=lam_r, in_=w_r), inc=vpre)
        yield
        W(A, "vector", vpre, 1)
        W(A, "scalar", vpre, 1)
        OP(A, "scalar", lambda e: e.activation(
            out=lnw_r, in_=w_r, func=AF.Ln), inc=sst)
        W(A, "sync", vpre, 1)
        DMA(A, "sync", lamb_t[:],
            bass.AP(rows, 16, [[128, 1], [0, P], [1, NH]]), vlam)
        yield
        W(A, "vector", sst, 2)
        yield from vy(lambda e: e.tensor_scalar(
            ll_r, u_r, 1.0 / 7.0, -1.0 / 6.0, ALU.mult, ALU.add))
        for c in (1.0 / 5.0, -1.0 / 4.0, 1.0 / 3.0, -1.0 / 2.0, 1.0):
            yield from vy(lambda e: e.tensor_tensor(ll_r, ll_r, u_r, ALU.mult))
            yield from vy(lambda e, c=c: e.tensor_scalar_add(ll_r, ll_r, c))
        yield from vy(lambda e: e.tensor_tensor(ll_r, ll_r, u_r, ALU.mult))
        yield from vy(lambda e: e.tensor_single_scalar(
            m_r, u_r, 0.25, ALU.is_lt))
        yield from vy(lambda e: e.tensor_tensor(ll_r, ll_r, m_r, ALU.mult))
        yield from vy(lambda e: e.tensor_scalar(
            m_r, m_r, -1.0, 1.0, ALU.mult, ALU.add))
        yield from vy(lambda e: e.tensor_tensor(m_r, m_r, lnw_r, ALU.mult))
        yield from vy(lambda e: e.tensor_tensor(ll_r, ll_r, m_r, ALU.add))
        yield from vy(lambda e: e.tensor_scalar_mul(ll_r, ll_r, -1.0))
        OP(A, "vector", lambda e: e.tensor_scalar_max(ll_r, ll_r, -87.0),
           inc=vll)
        yield
        W(A, "vector", vll, 1)
        W(A, "tensor", vll, 1)
        W(A, "tensor", vones)
        OP(A, "tensor", lambda e: e.matmul(
            llbP[:], onesr[:], ll_r, start=True, stop=True), inc=peE2)
        W(A, "vector", gio)
        yield from vy(lambda e: e.tensor_scalar(
            er2[:], er_col[:], -1.0, 127.0, ALU.mult, ALU.add))
        W(A, "vector", peE2, 1)
        OP(A, "vector", lambda e: e.tensor_scalar_mul(
            E1t[:], llbP[:], er2[:, 0:1]), inc=vE1)
        W(A, "scalar", vE1, 1)
        OP(A, "scalar", lambda e: e.activation(
            out=E1t[:], in_=E1t[:], func=AF.Exp), inc=sE1)
        yield
        yield from vy(lambda e: e.tensor_scalar(
            row1b[:], bpx[:], -128.0, 6016.0, ALU.mult, ALU.add))
        yield from vy(lambda e: e.tensor_copy(
            out=bass.AP(lltA, 0, [[NPB * NH, 1], [NH, NPB], [1, NH]]),
            in_=bass.AP(rows, 32, [[128, 1], [0, NPB], [1, NH]])))
        OP(A, "vector", lambda e: e.tensor_tensor(
            row1b[:], row1b[:], lltA[:], ALU.mult), inc=vrb)
        W(A, "scalar", vrb, 1)
        OP(A, "scalar", lambda e: e.activation(
            out=row1b[:], in_=row1b[:], func=AF.Exp), inc=sE2)
        yield
        W(A, "tensor", sE2, 1)
        W(A, "tensor", vE1, 1)
        OP(A, "tensor", lambda e: e.matmul(
            E2P[:], onesr[:], row1b[0:1, 0:512], start=True, stop=True),
            inc=peE2)
        W(A, "vector", peE2, 2)
        W(A, "vector", sE1, 1)
        OP(A, "vector", lambda e: e.tensor_tensor(
            omg_pre[:, 0:512],
            bass.AP(E1t, 0, [[NH, P], [0, 32], [1, NH]]),
            E2P[:], ALU.mult), inc=somg)
        yield
        W(A, "tensor", somg, 1)
        OP(A, "tensor", lambda e: e.matmul(
            E2P[:, 0:256], onesr[:], row1b[0:1, 512:768],
            start=True, stop=True), inc=peE2)
        W(A, "vector", peE2, 3)
        OP(A, "vector", lambda e: e.tensor_tensor(
            omg_pre[:, 512:768],
            bass.AP(E1t, 0, [[NH, P], [0, 16], [1, NH]]),
            E2P[:, 0:256], ALU.mult), inc=somg)

    wsA = PS[0:NH, 0:512]
    wsB = PS[0:NH, 512:1024]
    wsMu = PS[0:NH, 1024:1025]
    A2 = PS[:, 1536:2048]
    B2 = PS[:, 2048:2560]
    SSp = PS[0:1, 2560:3584]

    sst_at = {}
    mm_em = [0]

    def prior_stats():
        for bi in range(NXB):
            while bi >= 4 and mm_em[0] < bi - 3:
                yield
            par = bi % 4

            def bap(s, c0, c1, p=par):
                return bass.AP(xprev, p * XBB * D + s * D + c0,
                               [[4 * XBB * D, P], [1, c1 - c0]])
            W(A, "vector", xp, 16 * (bi + 1))
            for s in range(XBB):
                OP(A, "vector", lambda e, s=s, f=bap: e.bn_stats(
                    out=bnAll[:, s, 0:6], in_=f(s, 0, 512)))
                OP(A, "vector", lambda e, s=s, f=bap: e.bn_stats(
                    out=bnAll[:, s, 6:12], in_=f(s, 512, 1024)),
                    inc=(vst if s == XBB - 1 else None))
                if s % 2 == 1:
                    yield
            W(A, "vector", vst)
            for s in range(XBB):
                OP(A, "vector", lambda e, s=s: e.bn_aggr(
                    out=mv6[:, 2 * s:2 * s + 2], in_=bnAll[:, s, 0:12]),
                    inc=(vst if s == XBB - 1 else None))
            yield
            W(A, "vector", vst)
            OP(A, "vector", lambda e, bi=bi: e.tensor_copy(
                out=mu_all[:, bi * XBB:(bi + 1) * XBB],
                in_=bass.AP(mv6, 0, [[2 * XBB, P], [2, XBB]])))
            OP(A, "vector", lambda e: e.tensor_scalar_add(
                var6[:], bass.AP(mv6, 1, [[2 * XBB, P], [2, XBB]]), EPS),
                inc=vst)
            yield
            W(A, "vector", vst)
            OP(A, "vector", lambda e, bi=bi: e.reciprocal(
                out=rstd_all[:, bi * XBB:(bi + 1) * XBB], in_=var6[:]),
                inc=vst)
            W(A, "scalar", vst)
            OP(A, "scalar", lambda e, bi=bi: e.activation(
                out=rstd_all[:, bi * XBB:(bi + 1) * XBB],
                in_=rstd_all[:, bi * XBB:(bi + 1) * XBB],
                func=AF.Sqrt), inc=sst)
            sst_at[bi] = sst.n
            yield

    def prior_mms():
        for bi in range(NXB):
            while bi not in sst_at:
                yield
            par = bi % 4

            def bap(s, c0, c1, p=par):
                return bass.AP(xprev, p * XBB * D + s * D + c0,
                               [[4 * XBB * D, P], [1, c1 - c0]])
            W(A, "vector", sst, sst_at[bi])
            W(A, "vector", somg, 2)
            if bi >= 4:
                W(A, "vector", pewsum, bi - 3)
            OP(A, "vector", lambda e, bi=bi, par=par: e.tensor_tensor(
                omg_b[:, bi % 4, :],
                omg_pre[:, bi * XBB * NH:(bi + 1) * XBB * NH],
                bass.AP(rstd_all, bi * XBB, [[NPB, P], [1, XBB], [0, NH]]),
                ALU.mult), inc=vomg)
            yield
            W(A, "tensor", vomg, bi + 1)
            W(A, "tensor", xp, 16 * (bi + 1))
            for s in range(XBB):
                bp = bi * XBB + s
                st_, sp_ = bp == 0, bp == NPB - 1
                OP(A, "tensor", lambda e, f=bap, s=s, bi=bi, st_=st_,
                   sp_=sp_: e.matmul(
                       wsA, omg_b[:, bi % 4, s * NH:(s + 1) * NH],
                       f(s, 0, 512), start=st_, stop=sp_))
                OP(A, "tensor", lambda e, f=bap, s=s, bi=bi, st_=st_,
                   sp_=sp_: e.matmul(
                       wsB, omg_b[:, bi % 4, s * NH:(s + 1) * NH],
                       f(s, 512, 1024), start=st_, stop=sp_))
                OP(A, "tensor", lambda e, bp=bp, s=s, bi=bi, st_=st_,
                   sp_=sp_: e.matmul(
                       wsMu, omg_b[:, bi % 4, s * NH:(s + 1) * NH],
                       mu_all[:, bp:bp + 1], start=st_, stop=sp_),
                   inc=(pewsum if s == XBB - 1 else None))
            mm_em[0] = bi + 1
            yield

    def phase1_steps():
        for kc in range(FC):
            W(A, "vector", ldx, 16 * (kc + 1))
            OP(A, "vector", lambda e, kc=kc: e.tensor_copy(
                out=xn[:, kc, :], in_=x_full[:, kc, :]), inc=vxb)
            if kc % 2 == 1:
                yield
        vxq0 = sxq.n
        peSS0 = peSS.n
        W(A, "tensor", vones)
        for tt in range(NTT):
            if tt >= 1:
                W(A, "tensor", vrow1, tt)
            for kc in range(FC):
                gi = tt * FC + kc
                if tt == 0:
                    W(A, "scalar", ldx, 16 * (kc + 1))
                    W(A, "tensor", vxb, kc + 1)
                if gi >= 2:
                    W(A, "scalar", peSS, peSS0 + gi - 1)
                OP(A, "scalar", lambda e, kc=kc, tt=tt, b=gi % 2: e.activation(
                    out=xsq_t[:, b, :],
                    in_=x_full[:, kc, tt * TTS:(tt + 1) * TTS],
                    func=AF.Square), inc=sxq)
                W(A, "tensor", sxq, vxq0 + gi + 1)
                OP(A, "tensor", lambda e, kc=kc, tt=tt: e.matmul(
                    SSp[:, 0:512], onesc_bf[:],
                    xn[:, kc, tt * TTS:(tt + 1) * TTS],
                    start=(kc == 0), stop=(kc == FC - 1)))
                OP(A, "tensor", lambda e, kc=kc, b=gi % 2: e.matmul(
                    SSp[:, 512:1024], onesc_bf[:], xsq_t[:, b, :],
                    start=(kc == 0), stop=(kc == FC - 1)), inc=peSS)
                if kc % 2 == 1:
                    yield
            W(A, "vector", peSS, peSS0 + (tt + 1) * FC)
            mu_s = mrow[0:1, (tt % 2) * TTS:(tt % 2 + 1) * TTS]
            a_s = arow[0:1, (tt % 2) * TTS:(tt % 2 + 1) * TTS]
            b_s = brow[0:1, (tt % 2) * TTS:(tt % 2 + 1) * TTS]
            OP(A, "vector", lambda e, m=mu_s: e.tensor_scalar_mul(
                m, SSp[:, 0:512], 1.0 / D))
            OP(A, "vector", lambda e, a=a_s: e.tensor_scalar_mul(
                a, SSp[:, 512:1024], 1.0 / D))
            OP(A, "vector", lambda e, m=mu_s: e.tensor_mul(tmpA[0:1, :], m, m))
            OP(A, "vector", lambda e, a=a_s: e.tensor_sub(a, a, tmpA[0:1, :]))
            OP(A, "vector", lambda e, a=a_s: e.tensor_scalar_add(a, a, EPS))
            OP(A, "vector", lambda e, a=a_s: e.reciprocal(out=a, in_=a),
               inc=vrow1)
            W(A, "scalar", vrow1)
            OP(A, "scalar", lambda e, a=a_s: e.activation(
                out=a, in_=a, func=AF.Sqrt), inc=srow)
            yield
            W(A, "vector", srow)
            OP(A, "vector", lambda e, m=mu_s, a=a_s, b=b_s:
               e.scalar_tensor_tensor(b, m, -1.0, a, ALU.mult, ALU.mult),
               inc=vrow2)
            W(A, "tensor", vrow2)
            if tt >= 1:
                W(A, "tensor", vxn, tt)
            OP(A, "tensor", lambda e, a=a_s: e.matmul(
                A2, onesr[:], a, start=True, stop=True))
            OP(A, "tensor", lambda e, b=b_s: e.matmul(
                B2, onesr[:], b, start=True, stop=True), inc=peAB)
            W(A, "vector", peAB)
            for kc in range(FC):
                OP(A, "vector", lambda e, kc=kc, tt=tt: e.tensor_mul(
                    tmpA[:], x_full[:, kc, tt * TTS:(tt + 1) * TTS], A2))
                OP(A, "vector", lambda e, kc=kc, tt=tt: e.tensor_add(
                    xn[:, kc, tt * TTS:(tt + 1) * TTS], tmpA[:], B2),
                    inc=(vxn if kc == FC - 1 else None))
                if kc % 2 == 1:
                    yield

    g0, g1a, g1b, g2 = (lam_steps(), prior_stats(), prior_mms(),
                        phase1_steps())
    done1a = done2 = False
    for _ in g0:
        if not done1a:
            done1a = next(g1a, "END") == "END"
        if not done2:
            done2 = next(g2, "END") == "END"
    done1b = False
    while not (done1a and done1b and done2):
        if not done1a:
            done1a = next(g1a, "END") == "END"
        if not done1b:
            done1b = next(g1b, "END") == "END"
        if not done2:
            done2 = next(g2, "END") == "END"

    W(A, "vector", pewsum)
    OP(A, "vector", lambda e: e.tensor_copy(out=wmc[:], in_=wsMu), inc=vwm)
    W(A, "vector", vwm)
    OP(A, "vector", lambda e: e.tensor_scalar(
        wadj_bf[:, 0:512], wsA, wmc[:, 0:1], None, ALU.subtract))
    OP(A, "vector", lambda e: e.tensor_scalar(
        wadj_bf[:, 512:1024], wsB, wmc[:, 0:1], None, ALU.subtract), inc=vwadj)
    W(A, "sync", vwadj)
    for kc in range(FC):
        OP(A, "sync", lambda e, kc=kc: e.dma_start_transpose(
            wadjT[:, kc * NH:(kc + 1) * NH], wadj_bf[:, kc * P:(kc + 1) * P]),
            inc=tps, inck=16)

    o3_0 = PS[:, 3584:3584 + NH]
    o3_1 = PS[:, 1536:1536 + NH]
    W(A, "tensor", somg, 2)
    gwv0 = gwv.n
    for mc in range(FC):
        if mc >= 2:
            W(A, "gpsimd", peo3, mc - 1)
        DMA(A, "gpsimd", wvsA[:, mc % 2, :],
            bass.AP(wv_in, mc * P * D, [[D, P], [1, D]]), gwv)
    W(A, "tensor", tps)
    W(A, "tensor", vxn)
    for mc in range(FC):
        o3 = o3_0 if mc % 2 == 0 else o3_1
        W(A, "tensor", gwv, gwv0 + 16 * (mc + 1))
        if mc >= 2:
            W(A, "tensor", vco3, mc - 1)
        for kc in range(FC):
            OP(A, "tensor", lambda e, o3=o3, kc=kc, mc=mc: e.matmul(
                o3, wvsA[:, mc % 2, kc * P:(kc + 1) * P],
                wadjT[:, kc * NH:(kc + 1) * NH],
                start=(kc == 0), stop=(kc == FC - 1)),
                inc=(peo3 if kc == FC - 1 else None))
        W(A, "vector", peo3)
        OP(A, "vector", lambda e, o3=o3, mc=mc: e.tensor_copy(
            out=carry_fm[0:64, mc:mc + 1], in_=o3[0:64, 2 * mc:2 * mc + 1]))
        OP(A, "vector", lambda e, o3=o3, mc=mc: e.tensor_copy(
            out=carry_fm[64:128, mc:mc + 1],
            in_=o3[64:128, 2 * mc + 1:2 * mc + 2]), inc=vco3)

    gwv_preB = gwv.n
    for mc in range(2):
        DMA(A, "gpsimd", wsl[:, mc, :],
            bass.AP(wv_in, mc * P * D, [[D, P], [1, D]]), gwv)

    for eng in ENGS:
        OP(A, eng, lambda e: e.nop(), inc=barA)

    with nc.Block() as blkA:
        PRG[0].run(blkA)
    stA.close()

    B = 1
    stB = ExitStack()
    EB = stB.enter_context
    state = EB(nc.sbuf_tensor("state", [P, FC, TOK], BF16))
    ybf = EB(nc.sbuf_tensor("ybf", [P, FC, TOK], BF16))
    ysqb = EB(nc.sbuf_tensor("ysqb", [P, 2, FC, TTS], BF16))
    a2y_sb = EB(nc.sbuf_tensor("a2y_sb", [P, TOK], F32))
    b2y_sb = EB(nc.sbuf_tensor("b2y_sb", [P, TOK], F32))
    gs_sb = EB(nc.sbuf_tensor("gs_sb", [P, 8, TTS], BF16))
    attev = EB(nc.sbuf_tensor("attev", [P, 2, TTS], F32))
    tmpB = EB(nc.sbuf_tensor("tmpB", [P, 2, TTS], F32))
    t2b = EB(nc.sbuf_tensor("t2b", [P, 2, TTS], F32))

    for eng in ENGS:
        W(B, eng, barA)

    W(B, "vector", vlam, 16)
    for mc in range(FC):
        OP(B, "vector", lambda e, mc=mc: e.tensor_copy(
            out=lam_fm[0:64, mc:mc + 1],
            in_=lamb_t[0:64, 2 * mc:2 * mc + 1]))
        OP(B, "vector", lambda e, mc=mc: e.tensor_copy(
            out=lam_fm[64:128, mc:mc + 1],
            in_=lamb_t[64:128, 2 * mc + 1:2 * mc + 2]))

    pb = [PS[:, i * 512:(i + 1) * 512] for i in range(8)]

    def attn_stream(w_dr, cur_sem, cur0, prev_sem, prev_val,
                    base=None, skip=0):
        if base is None:
            base = gwv.n
        for mc in range(skip, FC):
            if mc < 2:
                W(B, "gpsimd", prev_sem, prev_val)
            else:
                W(B, "gpsimd", cur_sem, cur0 + mc - 1)
            DMA(B, "gpsimd", wsl[:, mc % 2, :],
                bass.AP(w_dr, mc * P * D, [[D, P], [1, D]]), gwv)
        return base

    gb = attn_stream(wv_in, peV, peV.n, barA, 0, base=gwv_preB, skip=2)
    for mc in range(FC):
        par = mc % 2
        W(B, "tensor", gwv, gb + 16 * (mc + 1))
        if mc >= 2:
            W(B, "tensor", vsc, mc - 1)
        for kc in range(FC):
            for tt in range(NTT):
                OP(B, "tensor", lambda e, mc=mc, kc=kc, tt=tt, par=par: e.matmul(
                    pb[par * 4 + tt], wsl[:, par, kc * P:(kc + 1) * P],
                    xn[:, kc, tt * TTS:(tt + 1) * TTS],
                    start=(kc == 0), stop=(kc == FC - 1)),
                    inc=(peV if (kc == FC - 1 and tt == NTT - 1) else None))
        W(B, "vector", peV)
        OP(B, "vector", lambda e, mc=mc, par=par: e.tensor_tensor_scan(
            state[:, mc, :],
            bass.AP(lam_fm, mc, [[FC, P], [0, TOK]]),
            PS[:, par * 2048:(par + 1) * 2048],
            carry_fm[:, mc:mc + 1], ALU.mult, ALU.add), inc=vsc)

    peQ0, vy0, peV_tot = peQ.n, vy.n, peV.n
    gb = attn_stream(wq_in, peQ, peQ0, peV, peV_tot)
    W(B, "tensor", vsc)
    for mc in range(FC):
        par = mc % 2
        W(B, "tensor", gwv, gb + 16 * (mc + 1))
        if mc >= 2:
            W(B, "tensor", vy, vy0 + mc - 1)
        for kc in range(FC):
            for tt in range(NTT):
                OP(B, "tensor", lambda e, mc=mc, kc=kc, tt=tt, par=par: e.matmul(
                    pb[par * 4 + tt], wsl[:, par, kc * P:(kc + 1) * P],
                    xn[:, kc, tt * TTS:(tt + 1) * TTS],
                    start=(kc == 0), stop=(kc == FC - 1)),
                    inc=(peQ if (kc == FC - 1 and tt == NTT - 1) else None))
        W(B, "vector", peQ)
        for tt in range(NTT):
            OP(B, "vector", lambda e, mc=mc, tt=tt, par=par: e.tensor_mul(
                ybf[:, mc, tt * TTS:(tt + 1) * TTS], pb[par * 4 + tt],
                state[:, mc, tt * TTS:(tt + 1) * TTS]),
                inc=(vy if tt == NTT - 1 else None))

    W(B, "tensor", vy)
    W(B, "scalar", vy)
    peSSy0c = peSSy.n
    for tt in range(NTT):
        par = tt % 2
        for mc in range(FC):
            if tt >= 2 and mc == 0:
                W(B, "scalar", peSSy, peSSy0c + tt - 1)
            OP(B, "scalar", lambda e, mc=mc, tt=tt, par=par: e.activation(
                out=ysqb[:, par, mc, :],
                in_=ybf[:, mc, tt * TTS:(tt + 1) * TTS], func=AF.Square),
                inc=(vys if mc == FC - 1 else None))
        W(B, "tensor", vys)
        if tt >= 2:
            W(B, "tensor", vyr1, tt - 1)
        sy = pb[par * 2]
        sy2 = pb[par * 2 + 1]
        for mc in range(FC):
            OP(B, "tensor", lambda e, S=sy, mc=mc, tt=tt: e.matmul(
                S[0:1, :], onesc_bf[:], ybf[:, mc, tt * TTS:(tt + 1) * TTS],
                start=(mc == 0), stop=(mc == FC - 1)))
            OP(B, "tensor", lambda e, S=sy2, mc=mc, par=par: e.matmul(
                S[0:1, :], onesc_bf[:], ysqb[:, par, mc, :],
                start=(mc == 0), stop=(mc == FC - 1)),
                inc=(peSSy if mc == FC - 1 else None))
        W(B, "vector", peSSy)
        mu_s = mrow[0:1, par * TTS:(par + 1) * TTS]
        a_s = arow[0:1, par * TTS:(par + 1) * TTS]
        b_s = brow[0:1, par * TTS:(par + 1) * TTS]
        OP(B, "vector", lambda e, S=sy, m=mu_s: e.tensor_scalar_mul(
            m, S[0:1, :], 1.0 / D))
        OP(B, "vector", lambda e, S=sy2, a=a_s: e.tensor_scalar_mul(
            a, S[0:1, :], 1.0 / D))
        OP(B, "vector", lambda e, m=mu_s: e.tensor_mul(tmpB[0:1, 0, :], m, m))
        OP(B, "vector", lambda e, a=a_s: e.tensor_sub(a, a, tmpB[0:1, 0, :]))
        OP(B, "vector", lambda e, a=a_s: e.tensor_scalar_add(a, a, EPS))
        OP(B, "vector", lambda e, a=a_s: e.reciprocal(out=a, in_=a), inc=vyr1)
        W(B, "scalar", vyr1)
        OP(B, "scalar", lambda e, a=a_s: e.activation(
            out=a, in_=a, func=AF.Sqrt), inc=syr)
        W(B, "vector", syr)
        OP(B, "vector", lambda e, m=mu_s, a=a_s, b=b_s: e.scalar_tensor_tensor(
            b, m, -1.0, a, ALU.mult, ALU.mult), inc=vyr2)
        A2 = pb[4 + par * 2]
        B2 = pb[5 + par * 2]
        W(B, "tensor", vyr2)
        if tt >= 2:
            W(B, "tensor", vab, tt - 1)
        OP(B, "tensor", lambda e, A2=A2, a=a_s: e.matmul(
            A2, onesr[:], a, start=True, stop=True))
        OP(B, "tensor", lambda e, B2=B2, b=b_s: e.matmul(
            B2, onesr[:], b, start=True, stop=True), inc=peABy)
        W(B, "vector", peABy)
        OP(B, "vector", lambda e, A2=A2, tt=tt: e.tensor_copy(
            out=a2y_sb[:, tt * TTS:(tt + 1) * TTS], in_=A2))
        OP(B, "vector", lambda e, B2=B2, tt=tt: e.tensor_copy(
            out=b2y_sb[:, tt * TTS:(tt + 1) * TTS], in_=B2), inc=vab)

    peG0, sgs0, vy20, peQ_tot = peG.n, sgs.n, vy2.n, peQ.n
    v2s0, s2v0 = v2s.n, s2v.n
    gb = attn_stream(wg_in, peG, peG0, peQ, peQ_tot)
    W(B, "tensor", vab)
    for mc in range(FC):
        par = mc % 2
        W(B, "tensor", gwv, gb + 16 * (mc + 1))
        if mc >= 2:
            W(B, "tensor", sgs, sgs0 + mc - 1)
        for kc in range(FC):
            for tt in range(NTT):
                OP(B, "tensor", lambda e, mc=mc, kc=kc, tt=tt, par=par: e.matmul(
                    pb[par * 4 + tt], wsl[:, par, kc * P:(kc + 1) * P],
                    xn[:, kc, tt * TTS:(tt + 1) * TTS],
                    start=(kc == 0), stop=(kc == FC - 1)),
                    inc=(peG if (kc == FC - 1 and tt == NTT - 1) else None))
        W(B, "scalar", peG)
        if mc >= 2:
            W(B, "scalar", vy2, vy20 + 4 * (mc - 1))
        for tt in range(NTT):
            OP(B, "scalar", lambda e, mc=mc, tt=tt, par=par: e.activation(
                out=gs_sb[:, par * 4 + tt, :], in_=pb[par * 4 + tt], func=AF.Silu),
                inc=(sgs if tt == NTT - 1 else None))
        W(B, "vector", sgs)

        def y2_pre(tt, mc=mc):
            idx = 4 * mc + tt
            sl = idx % 2
            if idx >= 2:
                W(B, "vector", s2v, s2v0 + idx - 1)
            OP(B, "vector", lambda e, mc=mc, tt=tt, sl=sl: e.tensor_mul(
                tmpB[:, sl, :], ybf[:, mc, tt * TTS:(tt + 1) * TTS],
                a2y_sb[:, tt * TTS:(tt + 1) * TTS]))
            OP(B, "vector", lambda e, tt=tt, sl=sl: e.tensor_add(
                tmpB[:, sl, :], tmpB[:, sl, :],
                b2y_sb[:, tt * TTS:(tt + 1) * TTS]), inc=v2s)
            W(B, "scalar", v2s, v2s0 + idx + 1)
            if idx >= 2:
                W(B, "scalar", vy2, vy20 + idx - 1)
            OP(B, "scalar", lambda e, mc=mc, sl=sl: e.activation(
                out=t2b[:, sl, :], in_=tmpB[:, sl, :], func=AF.Copy,
                scale=gam_t[:, mc:mc + 1]), inc=s2v)

        def y2_post(tt, mc=mc):
            idx = 4 * mc + tt
            sl = idx % 2
            W(B, "vector", s2v, s2v0 + idx + 1)
            OP(B, "vector", lambda e, mc=mc, tt=tt, sl=sl, par=mc % 2:
               e.scalar_tensor_tensor(
                   ybf[:, mc, tt * TTS:(tt + 1) * TTS], t2b[:, sl, :],
                   bet_t[:, mc:mc + 1], gs_sb[:, par * 4 + tt, :],
                   ALU.add, ALU.mult), inc=vy2)

        for tt in range(NTT):
            y2_pre(tt)
            if tt >= 1:
                y2_post(tt - 1)
        y2_post(NTT - 1)

    peWo0, peG_tot = peWo.n, peG.n
    gb = attn_stream(wo_in, peWo, peWo0, peG, peG_tot)
    W(B, "tensor", vy2)
    sp_i = 0
    for mc in range(FC):
        par = mc % 2
        W(B, "tensor", gwv, gb + 16 * (mc + 1))
        if mc >= 2:
            W(B, "tensor", sWo, (mc - 1) * NTT)
        for kc in range(FC):
            for tt in range(NTT):
                OP(B, "tensor", lambda e, mc=mc, kc=kc, tt=tt, par=par: e.matmul(
                    pb[par * 4 + tt], wsl[:, par, kc * P:(kc + 1) * P],
                    ybf[:, kc, tt * TTS:(tt + 1) * TTS],
                    start=(kc == 0), stop=(kc == FC - 1)),
                    inc=(peWo if (kc == FC - 1 and tt == NTT - 1) else None))
        W(B, "scalar", peWo)
        for tt in range(NTT):
            par2 = sp_i % 2
            if sp_i >= 2:
                W(B, "scalar", spl, 16 * (sp_i - 1))
            OP(B, "scalar", lambda e, tt=tt, par=par, par2=par2: e.copy(
                attev[:, par2, :], pb[par * 4 + tt]), inc=sWo)
            W(B, "sync", sWo)
            DMA(B, "sync",
                bass.AP(attn_sp, mc * TOK + tt * TTS, [[FC * TOK, P], [1, TTS]]),
                attev[:, par2, :], spl)
            sp_i += 1

    gw12_pre = gw12.n
    for mc in range(2):
        DMA(B, "gpsimd", w12b[:, mc, :],
            bass.AP(w12_in, mc * P * 2048, [[2048, P], [1, 2048]]), gw12)

    for eng in ENGS:
        OP(B, eng, lambda e: e.nop(), inc=barB)

    with nc.Block() as blkB:
        PRG[1].run(blkB)
    stB.close()
    mrow_cm.__exit__(None, None, None)
    brow_cm.__exit__(None, None, None)
    arow_cm.__exit__(None, None, None)

    C = 2
    stC = ExitStack()
    EC = stC.enter_context
    u_t = EC(nc.sbuf_tensor("u_t", [P, HC, 1024], BF16))
    h_t = EC(nc.sbuf_tensor("h_t", [P, HC, 1024], BF16))
    hbuf = EC(nc.sbuf_tensor("hbuf", [P, 3, 2048], BF16))
    u1sb = EC(nc.sbuf_tensor("u1sb", [P, 2, 1024], BF16))
    att_t = EC(nc.sbuf_tensor("att_t", [P, 2, TTS], F32))
    x_t = EC(nc.sbuf_tensor("x_t", [P, 2, TTS], F32))
    oB = EC(nc.sbuf_tensor("oB", [P, 2, TTS], F32))

    for eng in ENGS:
        W(C, eng, barB)
    W(C, "sync", spl)

    u1ps = [PS[:, 0:1024], PS[:, 1024:2048]]
    u2ps = [PS[:, 2048:3072], PS[:, 3072:4096]]
    hps = u1ps
    w3ps = u2ps

    jg_cell = [0]

    def emit_hloads(w_dr, pairs):
        for mc, hf in pairs:
            jg = jg_cell[0]
            jg_cell[0] += 1
            if jg >= 3:
                W(C, "gpsimd", peFHh, jg - 2)
            DMA(C, "gpsimd", hbuf[:, jg % 3, :],
                bass.AP(w_dr, mc * P * DH + hf * 2048,
                        [[DH, P], [1, 2048]]), gh)

    for p in range(2):
        TO = p * 1024

        vU0, sU0, peFU0 = vU.n, sU.n, peFU.n
        gw0 = gw12_pre if p == 0 else gw12.n
        h1_st = None
        W(C, "gpsimd", peFU)
        W(C, "tensor", vO)
        for mc in range(HC):
            par = mc % 2
            if mc == 2:
                h1_st = (jg_cell[0], gh.n)
                emit_hloads(h1_in, [(0, 0), (0, 1)])
            if not (p == 0 and mc < 2):
                if mc >= 2:
                    W(C, "gpsimd", peFU, peFU0 + mc - 1)
                DMA(C, "gpsimd", w12b[:, par, :],
                    bass.AP(w12_in, mc * P * 2048, [[2048, P], [1, 2048]]),
                    gw12)
            W(C, "tensor", gw12, gw0 + 16 * (mc + 1))
            if mc >= 2:
                W(C, "tensor", sU, sU0 + mc - 1)
                W(C, "tensor", vU, vU0 + mc - 1)
            for kc in range(FC):
                for ng in range(2):
                    OP(C, "tensor", lambda e, kc=kc, ng=ng, par=par, TO=TO:
                       e.matmul(
                           u1ps[par][:, ng * 512:(ng + 1) * 512],
                           w12b[:, par, kc * P:(kc + 1) * P],
                           xn[:, kc, TO + ng * 512:TO + (ng + 1) * 512],
                           start=(kc == 0), stop=(kc == FC - 1)))
                    OP(C, "tensor", lambda e, kc=kc, ng=ng, par=par, TO=TO:
                       e.matmul(
                           u2ps[par][:, ng * 512:(ng + 1) * 512],
                           w12b[:, par, D + kc * P: D + (kc + 1) * P],
                           xn[:, kc, TO + ng * 512:TO + (ng + 1) * 512],
                           start=(kc == 0), stop=(kc == FC - 1)),
                        inc=(peFU if (kc == FC - 1 and ng == 1) else None))
            W(C, "scalar", peFU)
            if mc >= 2:
                W(C, "scalar", vU, vU0 + mc - 1)
            OP(C, "scalar", lambda e, mc=mc, par=par: e.activation(
                out=u1sb[:, par, :], in_=u1ps[par][:], func=AF.Silu,
                scale=s1_t[:, mc:mc + 1]), inc=sU)
            W(C, "vector", sU)
            OP(C, "vector", lambda e, mc=mc, par=par: e.scalar_tensor_tensor(
                u_t[:, mc, :], u2ps[par][:], s2_t[:, mc:mc + 1], u1sb[:, par, :],
                ALU.mult, ALU.mult), inc=vU)

        def ffn_wide(w_dr, src_t, dst_t, scale_t, st=None):
            sH0 = sH.n
            if st is None:
                st = (jg_cell[0], gh.n)
                n_pre = 0
            else:
                n_pre = 2
            jg0, gh0 = st
            for mc in range(HC):
                for hf in range(2):
                    j = 2 * mc + hf
                    if j >= n_pre:
                        emit_hloads(w_dr, [(mc, hf)])
                par = mc % 2
                if mc >= 2:
                    W(C, "tensor", sH, sH0 + mc - 1)
                for kc in range(HC):
                    j = 2 * mc + kc // 16
                    if kc % 16 == 0:
                        W(C, "tensor", gh, gh0 + 16 * (j + 1))
                    for ng in range(2):
                        OP(C, "tensor",
                           lambda e, kc=kc, ng=ng, par=par, sl=(jg0 + j) % 3,
                           s=src_t: e.matmul(
                               hps[par][:, ng * 512:(ng + 1) * 512],
                               hbuf[:, sl, (kc % 16) * P:((kc % 16) + 1) * P],
                               s[:, kc, ng * 512:(ng + 1) * 512],
                               start=(kc == 0), stop=(kc == HC - 1)),
                           inc=(peFHh if (kc % 16 == 15 and ng == 1) else None))
                W(C, "scalar", peFHh)
                OP(C, "scalar", lambda e, mc=mc, par=par, d=dst_t, sc=scale_t:
                   e.activation(out=d[:, mc, :], in_=hps[par][:], func=AF.Gelu,
                                scale=sc[:, mc:mc + 1]), inc=sH)

        W(C, "tensor", vU)
        ffn_wide(h1_in, u_t, h_t, sh1_t, st=h1_st)
        W(C, "tensor", sH)
        ffn_wide(h2_in, h_t, u_t, sh2_t)

        vO0, sax0, sout0 = vO.n, sax.n, sout.n
        W(C, "tensor", sH)
        jg0w, gh0w = jg_cell[0], gh.n
        for mc in range(FC):
            emit_hloads(w3_in, [(mc, 0), (mc, 1)])
            par = mc % 2
            if mc >= 1:
                W(C, "sync", vO, vO0 + 2 * mc)
            for ng in range(2):
                DMA(C, "sync", att_t[:, ng, :],
                    bass.AP(attn_sp, mc * TOK + TO + ng * TTS,
                            [[FC * TOK, P], [1, TTS]]), sax)
                DMA(C, "sync", x_t[:, ng, :],
                    bass.AP(x_in, mc * TOK + TO + ng * TTS,
                            [[FC * TOK, P], [1, TTS]]), sax)
            if mc >= 2:
                W(C, "tensor", vO, vO0 + 2 * (mc - 1))
            for kc in range(HC):
                j = 2 * mc + kc // 16
                if kc % 16 == 0:
                    W(C, "tensor", gh, gh0w + 16 * (j + 1))
                for ng in range(2):
                    OP(C, "tensor", lambda e, kc=kc, ng=ng, par=par,
                       sl=(jg0w + j) % 3: e.matmul(
                           w3ps[par][:, ng * 512:(ng + 1) * 512],
                           hbuf[:, sl, (kc % 16) * P:((kc % 16) + 1) * P],
                           u_t[:, kc, ng * 512:(ng + 1) * 512],
                           start=(kc == 0), stop=(kc == HC - 1)),
                        inc=(peFHh if (kc % 16 == 15 and ng == 1) else None))
            W(C, "vector", peFHh)
            W(C, "vector", sax, sax0 + 64 * (mc + 1))
            if mc >= 1:
                W(C, "vector", sout, sout0 + 32 * mc)
            for ng in range(2):
                OP(C, "vector", lambda e, mc=mc, ng=ng, par=par:
                   e.scalar_tensor_tensor(
                       oB[:, ng, :], w3ps[par][:, ng * 512:(ng + 1) * 512],
                       s3_t[:, mc:mc + 1], att_t[:, ng, :], ALU.mult, ALU.add))
                OP(C, "vector", lambda e, ng=ng: e.tensor_add(
                    oB[:, ng, :], oB[:, ng, :], x_t[:, ng, :]), inc=vO)
                W(C, "sync", vO)
                DMA(C, "sync",
                    bass.AP(out_fm, mc * TOK + TO + ng * TTS,
                            [[FC * TOK, P], [1, TTS]]),
                    oB[:, ng, :], sout)

    W(C, "sync", sout)

    with nc.Block() as blkC:
        PRG[2].run(blkC)
    stC.close()
    xn_cm.__exit__(None, None, None)
    stC.close()
    ctx.close()
    nc.finalize()
    return nc


_NC_CACHE = None


def _get_nc():
    global _NC_CACHE
    if _NC_CACHE is None:
        _NC_CACHE = build_program()
    return _NC_CACHE


def _prep_weights(L0_attn_Wq, L0_attn_Wv, L0_attn_Wo, L0_attn_Wg,
                  L0_attn_beta, L0_attn_ln_gamma, L0_attn_ln_beta,
                  L0_ffn_w1_w, L0_ffn_w1_s, L0_ffn_w2_w, L0_ffn_w2_s,
                  L0_ffn_h1_w, L0_ffn_h1_s, L0_ffn_h2_w, L0_ffn_h2_s,
                  L0_ffn_w3_w, L0_ffn_w3_s):
    f32 = np.float32
    c = np.ascontiguousarray

    def tile_lhsT(WT, nmc, nkc):
        t = WT.reshape(nkc, P, nmc, P).transpose(2, 1, 0, 3)
        return c(t.reshape(nmc, P, nkc * P)).astype(f32)

    d = {}
    d["wq_in"] = tile_lhsT(np.asarray(L0_attn_Wq, f32).T, FC, FC)
    d["wv_in"] = tile_lhsT(np.asarray(L0_attn_Wv, f32).T, FC, FC)
    d["wg_in"] = tile_lhsT(np.asarray(L0_attn_Wg, f32).T, FC, FC)
    d["wo_in"] = tile_lhsT(np.asarray(L0_attn_Wo, f32).T, FC, FC)
    w1t = tile_lhsT(np.asarray(L0_ffn_w1_w, f32).T, HC, FC)
    w2t = tile_lhsT(np.asarray(L0_ffn_w2_w, f32).T, HC, FC)
    d["w12_in"] = c(np.concatenate([w1t, w2t], axis=2))
    d["h1_in"] = tile_lhsT(np.asarray(L0_ffn_h1_w, f32).T, HC, HC)
    d["h2_in"] = tile_lhsT(np.asarray(L0_ffn_h2_w, f32).T, HC, HC)
    d["w3_in"] = tile_lhsT(np.asarray(L0_ffn_w3_w, f32).T, FC, HC)
    d["s1_in"] = c(np.asarray(L0_ffn_w1_s, f32).reshape(HC, P).T)
    d["s2_in"] = c(np.asarray(L0_ffn_w2_s, f32).reshape(HC, P).T)
    d["sh1_in"] = c(np.asarray(L0_ffn_h1_s, f32).reshape(HC, P).T)
    d["sh2_in"] = c(np.asarray(L0_ffn_h2_s, f32).reshape(HC, P).T)
    d["s3_in"] = c(np.asarray(L0_ffn_w3_s, f32).reshape(FC, P).T)
    d["gam_in"] = c(np.asarray(L0_attn_ln_gamma, f32).reshape(FC, P).T)
    d["bet_in"] = c(np.asarray(L0_attn_ln_beta, f32).reshape(FC, P).T)
    d["beta_r"] = c(np.asarray(L0_attn_beta, f32).reshape(1, NH))
    return d


def kernel(L0_attn_x, L0_attn_Wq, L0_attn_Wv, L0_attn_Wo, L0_attn_Wg,
           L0_attn_beta, L0_attn_ln_gamma, L0_attn_ln_beta,
           L0_ffn_w1_w, L0_ffn_w1_s, L0_ffn_w2_w, L0_ffn_w2_s,
           L0_ffn_h1_w, L0_ffn_h1_s, L0_ffn_h2_w, L0_ffn_h2_s,
           L0_ffn_w3_w, L0_ffn_w3_s):
    x = np.asarray(L0_attn_x, np.float32)
    nc_ = _get_nc()
    wd = _prep_weights(L0_attn_Wq, L0_attn_Wv, L0_attn_Wo, L0_attn_Wg,
                       L0_attn_beta, L0_attn_ln_gamma, L0_attn_ln_beta,
                       L0_ffn_w1_w, L0_ffn_w1_s, L0_ffn_w2_w, L0_ffn_w2_s,
                       L0_ffn_h1_w, L0_ffn_h1_s, L0_ffn_h2_w, L0_ffn_h2_s,
                       L0_ffn_w3_w, L0_ffn_w3_s)

    in_maps = []
    for c_id in range(8):
        b, q = c_id // 4, c_id % 4
        xc = x[b, q * TOK:(q + 1) * TOK, :]
        x_fm = np.ascontiguousarray(
            xc.T.reshape(FC, P, TOK).transpose(1, 0, 2).reshape(P, FC * TOK))
        lo = q * TOK - PRI
        xpv = np.zeros((PRI, D), np.float32)
        if q > 0:
            xpv[max(0, -lo):] = x[b, max(0, lo):q * TOK, :]
        m = dict(wd)
        m["x_in"] = x_fm
        m["xp_in"] = xpv
        in_maps.append(m)

    res = run_bass_kernel_spmd(nc_, in_maps, core_ids=list(range(8)))

    out = np.empty_like(x)
    for c_id in range(8):
        b, q = c_id // 4, c_id % 4
        o = res.results[c_id]["out_fm"]
        out[b, q * TOK:(q + 1) * TOK, :] = \
            o.reshape(P, FC, TOK).transpose(1, 0, 2).reshape(D, TOK).T
    return out


# version 20
# speedup vs baseline: 1.0773x; 1.0108x over previous
import numpy as np
import ml_dtypes
from contextlib import ExitStack

import concourse.bass as bass
import concourse.mybir as mybir
from concourse.bass_utils import run_bass_kernel_spmd

F32 = mybir.dt.float32
BF16 = mybir.dt.bfloat16
AF = mybir.ActivationFunctionType
ALU = mybir.AluOpType

P = 128
TOK = 2048
NTT = 4
TTS = 512
FC = 8
HC = 32
D = 1024
DH = 4096
NH = 16
PRI = 6144
XBB = 6
NXB = 8
NPB = 48
EPS = 1e-5

ENGS = ("tensor", "vector", "scalar", "gpsimd", "sync")


class Prog:
    def __init__(self):
        self.q = {e: [] for e in ENGS}

    def emit(self, eng, fn):
        self.q[eng].append(fn)

    def run(self, block):
        for eng in ENGS:
            fns = self.q[eng]
            if not fns:
                continue
            getattr(block, eng)(lambda e, fns=fns: [fn(e) for fn in fns])


class Sem:
    def __init__(self, stack, nc, name):
        self.h = stack.enter_context(nc.semaphore(name))
        self.n = 0


def build_program():
    nc = bass.Bass("TRN2", target_bir_lowering=False, detect_race_conditions=False)

    x_in = nc.dram_tensor("x_in", [P, FC * TOK], F32, kind="ExternalInput")
    xp_in = nc.dram_tensor("xp_in", [PRI, D], F32, kind="ExternalInput")
    beta_r = nc.dram_tensor("beta_r", [1, NH], F32, kind="ExternalInput")
    wq_in = nc.dram_tensor("wq_in", [FC, P, D], F32, kind="ExternalInput")
    wv_in = nc.dram_tensor("wv_in", [FC, P, D], F32, kind="ExternalInput")
    wg_in = nc.dram_tensor("wg_in", [FC, P, D], F32, kind="ExternalInput")
    wo_in = nc.dram_tensor("wo_in", [FC, P, D], F32, kind="ExternalInput")
    gam_in = nc.dram_tensor("gam_in", [P, FC], F32, kind="ExternalInput")
    bet_in = nc.dram_tensor("bet_in", [P, FC], F32, kind="ExternalInput")
    w12_in = nc.dram_tensor("w12_in", [HC, P, 2048], F32, kind="ExternalInput")
    h1_in = nc.dram_tensor("h1_in", [HC, P, DH], F32, kind="ExternalInput")
    h2_in = nc.dram_tensor("h2_in", [HC, P, DH], F32, kind="ExternalInput")
    w3_in = nc.dram_tensor("w3_in", [FC, P, DH], F32, kind="ExternalInput")
    s1_in = nc.dram_tensor("s1_in", [P, HC], F32, kind="ExternalInput")
    s2_in = nc.dram_tensor("s2_in", [P, HC], F32, kind="ExternalInput")
    sh1_in = nc.dram_tensor("sh1_in", [P, HC], F32, kind="ExternalInput")
    sh2_in = nc.dram_tensor("sh2_in", [P, HC], F32, kind="ExternalInput")
    s3_in = nc.dram_tensor("s3_in", [P, FC], F32, kind="ExternalInput")

    out_fm = nc.dram_tensor("out_fm", [P, FC * TOK], F32, kind="ExternalOutput")
    attn_sp = nc.dram_tensor("attn_sp", [P, FC * TOK], F32, kind="Internal")
    xn_sp = nc.dram_tensor("xn_sp", [P, FC * TOK], BF16, kind="Internal")

    ctx = ExitStack()
    E = ctx.enter_context

    s1_t = E(nc.sbuf_tensor("s1_t", [P, HC], F32))
    s2_t = E(nc.sbuf_tensor("s2_t", [P, HC], F32))
    sh1_t = E(nc.sbuf_tensor("sh1_t", [P, HC], F32))
    sh2_t = E(nc.sbuf_tensor("sh2_t", [P, HC], F32))
    s3_t = E(nc.sbuf_tensor("s3_t", [P, FC], F32))
    gam_t = E(nc.sbuf_tensor("gam_t", [P, FC], F32))
    bet_t = E(nc.sbuf_tensor("bet_t", [P, FC], F32))
    beta_sb = E(nc.sbuf_tensor("beta_sb", [1, NH], F32))
    rows = E(nc.sbuf_tensor("rows", [1, 128], F32))
    erow = E(nc.sbuf_tensor("erow", [1, P], F32))
    onesr = E(nc.sbuf_tensor("onesr", [1, P], F32))
    onesc_bf = E(nc.sbuf_tensor("onesc_bf", [P, 1], BF16))

    carry_fm = E(nc.sbuf_tensor("carry_fm", [P, FC], F32))
    lam_fm = E(nc.sbuf_tensor("lam_fm", [P, FC], F32))
    w12b = E(nc.sbuf_tensor("w12b", [P, 2, 2048], BF16))
    wsl = E(nc.sbuf_tensor("wsl", [P, 2, D], BF16))
    PS = E(nc.psum_tensor("PS", [P, 4096], F32))

    xn_cm = nc.sbuf_tensor("xn", [P, FC, TOK], BF16)
    xn = xn_cm.__enter__()
    arow_cm = nc.sbuf_tensor("arow", [1, 1024], F32)
    arow = arow_cm.__enter__()
    brow_cm = nc.sbuf_tensor("brow", [1, 1024], F32)
    brow = brow_cm.__enter__()
    mrow_cm = nc.sbuf_tensor("mrow", [1, 1024], F32)
    mrow = mrow_cm.__enter__()

    sstk = ExitStack()
    ctx.enter_context(sstk)

    def mks(name):
        return Sem(sstk, nc, name)

    ld = mks("ld")
    ldx = mks("ldx")
    gwv = mks("gwv")
    xp = mks("xp")
    vpre = mks("vpre")
    gio = mks("gio")
    peT1 = mks("peT1")
    somg = mks("somg")
    vlam = mks("vlam")
    vst = mks("vst")
    sst = mks("sst")
    vomg = mks("vomg")
    pewsum = mks("pewsum")
    vwadj = mks("vwadj")
    vwm = mks("vwm")
    vch = mks("vch")
    tps = mks("tps")
    peo3 = mks("peo3")
    vco3 = mks("vco3")
    vxb = mks("vxb")
    vxq = mks("vxq")
    sxq = mks("sxq")
    peE2 = mks("peE2")
    peSum = mks("peSum")
    vev = mks("vev")
    sAB = mks("sAB")
    vll = mks("vll")
    vrb = mks("vrb")
    vE1 = mks("vE1")
    sE1 = mks("sE1")
    sE2 = mks("sE2")
    gllb = mks("gllb")
    gE2 = mks("gE2")
    vones = mks("vones")
    peSS = mks("peSS")
    vrow1 = mks("vrow1")
    srow = mks("srow")
    vrow2 = mks("vrow2")
    peAB = mks("peAB")
    vxn = mks("vxn")
    barA = mks("barA")
    peV = mks("peV")
    vsc = mks("vsc")
    peQ = mks("peQ")
    vy = mks("vy")
    vys = mks("vys")
    peSSy = mks("peSSy")
    vyr1 = mks("vyr1")
    syr = mks("syr")
    vyr2 = mks("vyr2")
    peABy = mks("peABy")
    vab = mks("vab")
    peG = mks("peG")
    sgs = mks("sgs")
    vy2 = mks("vy2")
    v2s = mks("v2s")
    s2v = mks("s2v")
    peWo = mks("peWo")
    sWo = mks("sWo")
    spl = mks("spl")
    sxn = mks("sxn")
    barB = mks("barB")
    gxn = mks("gxn")
    gw12 = mks("gw12")
    peFU = mks("peFU")
    sU = mks("sU")
    vU = mks("vU")
    gh = mks("gh")
    peFHh = mks("peFHh")
    sH = mks("sH")
    sax = mks("sax")
    vO = mks("vO")
    sout = mks("sout")

    PRG = [Prog(), Prog(), Prog()]

    def W(blk, eng, sem, val=None):
        v = sem.n if val is None else val
        if v <= 0:
            return
        PRG[blk].emit(eng, lambda e, s=sem.h, v=v: e.wait_ge(s, v))

    def OP(blk, eng, fn, inc=None, inck=1):
        if inc is not None:
            inc.n += inck
        def run(e, fn=fn, inc=inc, inck=inck):
            inst = fn(e)
            if inc is not None:
                inst.then_inc(inc.h, inck)
            return inst
        PRG[blk].emit(eng, run)

    def VOP(blk, fn):
        OP(blk, "vector", fn, inc=vch)
        W(blk, "vector", vch)

    def DMA(blk, eng, out_ap, in_ap, inc):
        OP(blk, eng, lambda e, o=out_ap, i=in_ap: e.dma_start(out=o, in_=i),
           inc=inc, inck=16)

    stA = ExitStack()
    EA = stA.enter_context
    x_full = EA(nc.sbuf_tensor("x_full", [P, FC, TOK], F32))
    xprev = EA(nc.sbuf_tensor("xprev", [P, 4, XBB, D], BF16))
    xsq_t = EA(nc.sbuf_tensor("xsq_t", [P, 2, TTS], BF16))
    tmpA = EA(nc.sbuf_tensor("tmpA", [P, TTS], F32))
    wmc = EA(nc.sbuf_tensor("wmc", [NH, 1], F32))
    wvsA = EA(nc.sbuf_tensor("wvsA", [P, 2, D], BF16))
    lltA = EA(nc.sbuf_tensor("lltA", [1, NPB * NH], F32))
    row1b = EA(nc.sbuf_tensor("row1b", [1, NPB * NH], F32))
    bpx = EA(nc.sbuf_tensor("bpx", [1, NPB * NH], F32))
    t1s = EA(nc.sbuf_tensor("t1s", [P, NPB * NH], F32))
    omg_pre = EA(nc.sbuf_tensor("omg_pre", [P, NPB * NH], BF16))
    omg_b = EA(nc.sbuf_tensor("omg_b", [P, 4, XBB * NH], BF16))
    mu_all = EA(nc.sbuf_tensor("mu_all", [P, NPB], BF16))
    rstd_all = EA(nc.sbuf_tensor("rstd_all", [P, NPB], F32))
    bnAll = EA(nc.sbuf_tensor("bnAll", [P, XBB, 12], F32))
    mv6 = EA(nc.sbuf_tensor("mv6", [P, 2 * XBB], F32))
    var6 = EA(nc.sbuf_tensor("var6", [P, XBB], F32))
    wadj_bf = EA(nc.sbuf_tensor("wadj_bf", [NH, D], BF16))
    wadjT = EA(nc.sbuf_tensor("wadjT", [P, FC * NH], BF16))
    E1t = EA(nc.sbuf_tensor("E1t", [P, NH], F32))
    er_col = EA(nc.sbuf_tensor("er_col", [P, 1], F32))
    er2 = EA(nc.sbuf_tensor("er2", [P, 1], F32))
    lamb_t = EA(nc.sbuf_tensor("lamb_t", [P, NH], F32))

    A = 0
    for t_sb, t_dr in ((s1_t, s1_in), (s2_t, s2_in), (sh1_t, sh1_in),
                       (sh2_t, sh2_in), (s3_t, s3_in), (gam_t, gam_in),
                       (bet_t, bet_in), (beta_sb, beta_r)):
        DMA(A, "sync", t_sb[:], t_dr[:], ld)
    for kc in range(FC):
        DMA(A, "sync", x_full[:, kc, :],
            bass.AP(x_in, kc * TOK, [[FC * TOK, P], [1, TOK]]), ldx)

    OP(A, "gpsimd", lambda e: e.iota(
        bpx[:], [[1, NPB], [0, NH]], channel_multiplier=0,
        allow_small_or_imprecise_dtypes=True))
    OP(A, "gpsimd", lambda e: e.iota(
        er_col[:], [[0, 1]], channel_multiplier=1,
        allow_small_or_imprecise_dtypes=True), inc=gio)
    for bi in range(NXB):
        if bi >= 4:
            W(A, "gpsimd", pewsum, bi - 3)
        par = bi % 4
        in_ap = bass.AP(xp_in, bi * XBB * P * D,
                        [[D, P], [P * D, XBB], [1, D]])
        out_ap = bass.AP(xprev, par * XBB * D,
                         [[4 * XBB * D, P], [D, XBB], [1, D]])
        DMA(A, "gpsimd", out_ap, in_ap, xp)

    u_r = rows[0:1, 0:16]
    lam_r = rows[0:1, 16:32]
    ll_r = rows[0:1, 32:48]
    w_r = rows[0:1, 48:64]
    lnw_r = rows[0:1, 64:80]
    m_r = rows[0:1, 80:96]

    llbP = PS[:, 3584:3600]
    E2P = PS[:, 3584:4096]

    OP(A, "vector", lambda e: e.memset(onesr[:], 1.0))
    OP(A, "vector", lambda e: e.memset(onesc_bf[:], 1.0), inc=vones)

    def lam_steps():
        def vy(fn):
            OP(A, "vector", fn, inc=vch)
            w = vch.n
            yield
            W(A, "vector", vch, w)
        W(A, "scalar", ld)
        OP(A, "scalar", lambda e: e.activation(
            out=u_r, in_=beta_sb[:], func=AF.Exp, scale=-1.0), inc=sst)
        yield
        W(A, "vector", sst, 1)
        yield from vy(lambda e: e.tensor_scalar_add(w_r, u_r, 1.0))
        OP(A, "vector", lambda e: e.reciprocal(out=lam_r, in_=w_r), inc=vpre)
        yield
        W(A, "vector", vpre, 1)
        W(A, "scalar", vpre, 1)
        OP(A, "scalar", lambda e: e.activation(
            out=lnw_r, in_=w_r, func=AF.Ln), inc=sst)
        W(A, "sync", vpre, 1)
        DMA(A, "sync", lamb_t[:],
            bass.AP(rows, 16, [[128, 1], [0, P], [1, NH]]), vlam)
        yield
        W(A, "vector", sst, 2)
        yield from vy(lambda e: e.tensor_scalar(
            ll_r, u_r, 1.0 / 7.0, -1.0 / 6.0, ALU.mult, ALU.add))
        for c in (1.0 / 5.0, -1.0 / 4.0, 1.0 / 3.0, -1.0 / 2.0, 1.0):
            yield from vy(lambda e: e.tensor_tensor(ll_r, ll_r, u_r, ALU.mult))
            yield from vy(lambda e, c=c: e.tensor_scalar_add(ll_r, ll_r, c))
        yield from vy(lambda e: e.tensor_tensor(ll_r, ll_r, u_r, ALU.mult))
        yield from vy(lambda e: e.tensor_single_scalar(
            m_r, u_r, 0.25, ALU.is_lt))
        yield from vy(lambda e: e.tensor_tensor(ll_r, ll_r, m_r, ALU.mult))
        yield from vy(lambda e: e.tensor_scalar(
            m_r, m_r, -1.0, 1.0, ALU.mult, ALU.add))
        yield from vy(lambda e: e.tensor_tensor(m_r, m_r, lnw_r, ALU.mult))
        yield from vy(lambda e: e.tensor_tensor(ll_r, ll_r, m_r, ALU.add))
        yield from vy(lambda e: e.tensor_scalar_mul(ll_r, ll_r, -1.0))
        OP(A, "vector", lambda e: e.tensor_scalar_max(ll_r, ll_r, -87.0),
           inc=vll)
        yield
        W(A, "vector", vll, 1)
        W(A, "tensor", vll, 1)
        W(A, "tensor", vones)
        OP(A, "tensor", lambda e: e.matmul(
            llbP[:], onesr[:], ll_r, start=True, stop=True), inc=peE2)
        W(A, "vector", gio)
        yield from vy(lambda e: e.tensor_scalar(
            er2[:], er_col[:], -1.0, 127.0, ALU.mult, ALU.add))
        W(A, "vector", peE2, 1)
        OP(A, "vector", lambda e: e.tensor_scalar_mul(
            E1t[:], llbP[:], er2[:, 0:1]), inc=vE1)
        W(A, "scalar", vE1, 1)
        OP(A, "scalar", lambda e: e.activation(
            out=E1t[:], in_=E1t[:], func=AF.Exp), inc=sE1)
        yield
        yield from vy(lambda e: e.tensor_scalar(
            row1b[:], bpx[:], -128.0, 6016.0, ALU.mult, ALU.add))
        yield from vy(lambda e: e.tensor_copy(
            out=bass.AP(lltA, 0, [[NPB * NH, 1], [NH, NPB], [1, NH]]),
            in_=bass.AP(rows, 32, [[128, 1], [0, NPB], [1, NH]])))
        OP(A, "vector", lambda e: e.tensor_tensor(
            row1b[:], row1b[:], lltA[:], ALU.mult), inc=vrb)
        W(A, "scalar", vrb, 1)
        OP(A, "scalar", lambda e: e.activation(
            out=row1b[:], in_=row1b[:], func=AF.Exp), inc=sE2)
        yield
        W(A, "tensor", sE2, 1)
        W(A, "tensor", vE1, 1)
        OP(A, "tensor", lambda e: e.matmul(
            E2P[:], onesr[:], row1b[0:1, 0:512], start=True, stop=True),
            inc=peE2)
        W(A, "vector", peE2, 2)
        W(A, "vector", sE1, 1)
        OP(A, "vector", lambda e: e.tensor_tensor(
            omg_pre[:, 0:512],
            bass.AP(E1t, 0, [[NH, P], [0, 32], [1, NH]]),
            E2P[:], ALU.mult), inc=somg)
        yield
        W(A, "tensor", somg, 1)
        OP(A, "tensor", lambda e: e.matmul(
            E2P[:, 0:256], onesr[:], row1b[0:1, 512:768],
            start=True, stop=True), inc=peE2)
        W(A, "vector", peE2, 3)
        OP(A, "vector", lambda e: e.tensor_tensor(
            omg_pre[:, 512:768],
            bass.AP(E1t, 0, [[NH, P], [0, 16], [1, NH]]),
            E2P[:, 0:256], ALU.mult), inc=somg)

    wsA = PS[0:NH, 0:512]
    wsB = PS[0:NH, 512:1024]
    wsMu = PS[0:NH, 1024:1025]
    A2 = PS[:, 1536:2048]
    B2 = PS[:, 2048:2560]
    SSp = PS[0:1, 2560:3584]

    sst_at = {}
    mm_em = [0]

    def prior_stats():
        for bi in range(NXB):
            while bi >= 4 and mm_em[0] < bi - 3:
                yield
            par = bi % 4

            def bap(s, c0, c1, p=par):
                return bass.AP(xprev, p * XBB * D + s * D + c0,
                               [[4 * XBB * D, P], [1, c1 - c0]])
            W(A, "vector", xp, 16 * (bi + 1))
            for s in range(XBB):
                OP(A, "vector", lambda e, s=s, f=bap: e.bn_stats(
                    out=bnAll[:, s, 0:6], in_=f(s, 0, 512)))
                OP(A, "vector", lambda e, s=s, f=bap: e.bn_stats(
                    out=bnAll[:, s, 6:12], in_=f(s, 512, 1024)),
                    inc=(vst if s == XBB - 1 else None))
                if s % 2 == 1:
                    yield
            W(A, "vector", vst)
            for s in range(XBB):
                OP(A, "vector", lambda e, s=s: e.bn_aggr(
                    out=mv6[:, 2 * s:2 * s + 2], in_=bnAll[:, s, 0:12]),
                    inc=(vst if s == XBB - 1 else None))
            yield
            W(A, "vector", vst)
            OP(A, "vector", lambda e, bi=bi: e.tensor_copy(
                out=mu_all[:, bi * XBB:(bi + 1) * XBB],
                in_=bass.AP(mv6, 0, [[2 * XBB, P], [2, XBB]])))
            OP(A, "vector", lambda e: e.tensor_scalar_add(
                var6[:], bass.AP(mv6, 1, [[2 * XBB, P], [2, XBB]]), EPS),
                inc=vst)
            yield
            W(A, "vector", vst)
            OP(A, "vector", lambda e, bi=bi: e.reciprocal(
                out=rstd_all[:, bi * XBB:(bi + 1) * XBB], in_=var6[:]),
                inc=vst)
            W(A, "scalar", vst)
            OP(A, "scalar", lambda e, bi=bi: e.activation(
                out=rstd_all[:, bi * XBB:(bi + 1) * XBB],
                in_=rstd_all[:, bi * XBB:(bi + 1) * XBB],
                func=AF.Sqrt), inc=sst)
            sst_at[bi] = sst.n
            yield

    def prior_mms():
        for bi in range(NXB):
            while bi not in sst_at:
                yield
            par = bi % 4

            def bap(s, c0, c1, p=par):
                return bass.AP(xprev, p * XBB * D + s * D + c0,
                               [[4 * XBB * D, P], [1, c1 - c0]])
            W(A, "vector", sst, sst_at[bi])
            W(A, "vector", somg, 2)
            if bi >= 4:
                W(A, "vector", pewsum, bi - 3)
            OP(A, "vector", lambda e, bi=bi, par=par: e.tensor_tensor(
                omg_b[:, bi % 4, :],
                omg_pre[:, bi * XBB * NH:(bi + 1) * XBB * NH],
                bass.AP(rstd_all, bi * XBB, [[NPB, P], [1, XBB], [0, NH]]),
                ALU.mult), inc=vomg)
            yield
            W(A, "tensor", vomg, bi + 1)
            W(A, "tensor", xp, 16 * (bi + 1))
            for s in range(XBB):
                bp = bi * XBB + s
                st_, sp_ = bp == 0, bp == NPB - 1
                OP(A, "tensor", lambda e, f=bap, s=s, bi=bi, st_=st_,
                   sp_=sp_: e.matmul(
                       wsA, omg_b[:, bi % 4, s * NH:(s + 1) * NH],
                       f(s, 0, 512), start=st_, stop=sp_))
                OP(A, "tensor", lambda e, f=bap, s=s, bi=bi, st_=st_,
                   sp_=sp_: e.matmul(
                       wsB, omg_b[:, bi % 4, s * NH:(s + 1) * NH],
                       f(s, 512, 1024), start=st_, stop=sp_))
                OP(A, "tensor", lambda e, bp=bp, s=s, bi=bi, st_=st_,
                   sp_=sp_: e.matmul(
                       wsMu, omg_b[:, bi % 4, s * NH:(s + 1) * NH],
                       mu_all[:, bp:bp + 1], start=st_, stop=sp_),
                   inc=(pewsum if s == XBB - 1 else None))
            mm_em[0] = bi + 1
            yield

    def phase1_steps():
        for kc in range(FC):
            W(A, "vector", ldx, 16 * (kc + 1))
            OP(A, "vector", lambda e, kc=kc: e.tensor_copy(
                out=xn[:, kc, :], in_=x_full[:, kc, :]), inc=vxb)
            if kc % 2 == 1:
                yield
        vxq0 = sxq.n
        peSS0 = peSS.n
        W(A, "tensor", vones)
        for tt in range(NTT):
            if tt >= 1:
                W(A, "tensor", vrow1, tt)
            for kc in range(FC):
                gi = tt * FC + kc
                if tt == 0:
                    W(A, "scalar", ldx, 16 * (kc + 1))
                    W(A, "tensor", vxb, kc + 1)
                if gi >= 2:
                    W(A, "scalar", peSS, peSS0 + gi - 1)
                OP(A, "scalar", lambda e, kc=kc, tt=tt, b=gi % 2: e.activation(
                    out=xsq_t[:, b, :],
                    in_=x_full[:, kc, tt * TTS:(tt + 1) * TTS],
                    func=AF.Square), inc=sxq)
                W(A, "tensor", sxq, vxq0 + gi + 1)
                OP(A, "tensor", lambda e, kc=kc, tt=tt: e.matmul(
                    SSp[:, 0:512], onesc_bf[:],
                    xn[:, kc, tt * TTS:(tt + 1) * TTS],
                    start=(kc == 0), stop=(kc == FC - 1)))
                OP(A, "tensor", lambda e, kc=kc, b=gi % 2: e.matmul(
                    SSp[:, 512:1024], onesc_bf[:], xsq_t[:, b, :],
                    start=(kc == 0), stop=(kc == FC - 1)), inc=peSS)
                if kc % 2 == 1:
                    yield
            W(A, "vector", peSS, peSS0 + (tt + 1) * FC)
            mu_s = mrow[0:1, (tt % 2) * TTS:(tt % 2 + 1) * TTS]
            a_s = arow[0:1, (tt % 2) * TTS:(tt % 2 + 1) * TTS]
            b_s = brow[0:1, (tt % 2) * TTS:(tt % 2 + 1) * TTS]
            OP(A, "vector", lambda e, m=mu_s: e.tensor_scalar_mul(
                m, SSp[:, 0:512], 1.0 / D))
            OP(A, "vector", lambda e, a=a_s: e.tensor_scalar_mul(
                a, SSp[:, 512:1024], 1.0 / D))
            OP(A, "vector", lambda e, m=mu_s: e.tensor_mul(tmpA[0:1, :], m, m))
            OP(A, "vector", lambda e, a=a_s: e.tensor_sub(a, a, tmpA[0:1, :]))
            OP(A, "vector", lambda e, a=a_s: e.tensor_scalar_add(a, a, EPS))
            OP(A, "vector", lambda e, a=a_s: e.reciprocal(out=a, in_=a),
               inc=vrow1)
            W(A, "scalar", vrow1)
            OP(A, "scalar", lambda e, a=a_s: e.activation(
                out=a, in_=a, func=AF.Sqrt), inc=srow)
            yield
            W(A, "vector", srow)
            OP(A, "vector", lambda e, m=mu_s, a=a_s, b=b_s:
               e.scalar_tensor_tensor(b, m, -1.0, a, ALU.mult, ALU.mult),
               inc=vrow2)
            W(A, "tensor", vrow2)
            if tt >= 1:
                W(A, "tensor", vxn, tt)
            OP(A, "tensor", lambda e, a=a_s: e.matmul(
                A2, onesr[:], a, start=True, stop=True))
            OP(A, "tensor", lambda e, b=b_s: e.matmul(
                B2, onesr[:], b, start=True, stop=True), inc=peAB)
            W(A, "vector", peAB)
            for kc in range(FC):
                OP(A, "vector", lambda e, kc=kc, tt=tt: e.tensor_mul(
                    tmpA[:], x_full[:, kc, tt * TTS:(tt + 1) * TTS], A2))
                OP(A, "vector", lambda e, kc=kc, tt=tt: e.tensor_add(
                    xn[:, kc, tt * TTS:(tt + 1) * TTS], tmpA[:], B2),
                    inc=(vxn if kc == FC - 1 else None))
                if kc % 2 == 1:
                    yield

    g0, g1a, g1b, g2 = (lam_steps(), prior_stats(), prior_mms(),
                        phase1_steps())
    done1a = done2 = False
    for _ in g0:
        if not done1a:
            done1a = next(g1a, "END") == "END"
        if not done2:
            done2 = next(g2, "END") == "END"
    done1b = False
    while not (done1a and done1b and done2):
        if not done1a:
            done1a = next(g1a, "END") == "END"
        if not done1b:
            done1b = next(g1b, "END") == "END"
        if not done2:
            done2 = next(g2, "END") == "END"

    W(A, "vector", pewsum)
    OP(A, "vector", lambda e: e.tensor_copy(out=wmc[:], in_=wsMu), inc=vwm)
    W(A, "vector", vwm)
    OP(A, "vector", lambda e: e.tensor_scalar(
        wadj_bf[:, 0:512], wsA, wmc[:, 0:1], None, ALU.subtract))
    OP(A, "vector", lambda e: e.tensor_scalar(
        wadj_bf[:, 512:1024], wsB, wmc[:, 0:1], None, ALU.subtract), inc=vwadj)
    W(A, "sync", vwadj)
    for kc in range(FC):
        OP(A, "sync", lambda e, kc=kc: e.dma_start_transpose(
            wadjT[:, kc * NH:(kc + 1) * NH], wadj_bf[:, kc * P:(kc + 1) * P]),
            inc=tps, inck=16)

    o3_0 = PS[:, 3584:3584 + NH]
    o3_1 = PS[:, 1536:1536 + NH]
    W(A, "tensor", somg, 2)
    gwv0 = gwv.n
    for mc in range(FC):
        if mc >= 2:
            W(A, "gpsimd", peo3, mc - 1)
        DMA(A, "gpsimd", wvsA[:, mc % 2, :],
            bass.AP(wv_in, mc * P * D, [[D, P], [1, D]]), gwv)
    W(A, "tensor", tps)
    W(A, "tensor", vxn)
    for mc in range(FC):
        o3 = o3_0 if mc % 2 == 0 else o3_1
        W(A, "tensor", gwv, gwv0 + 16 * (mc + 1))
        if mc >= 2:
            W(A, "tensor", vco3, mc - 1)
        for kc in range(FC):
            OP(A, "tensor", lambda e, o3=o3, kc=kc, mc=mc: e.matmul(
                o3, wvsA[:, mc % 2, kc * P:(kc + 1) * P],
                wadjT[:, kc * NH:(kc + 1) * NH],
                start=(kc == 0), stop=(kc == FC - 1)),
                inc=(peo3 if kc == FC - 1 else None))
        W(A, "vector", peo3)
        OP(A, "vector", lambda e, o3=o3, mc=mc: e.tensor_copy(
            out=carry_fm[0:64, mc:mc + 1], in_=o3[0:64, 2 * mc:2 * mc + 1]))
        OP(A, "vector", lambda e, o3=o3, mc=mc: e.tensor_copy(
            out=carry_fm[64:128, mc:mc + 1],
            in_=o3[64:128, 2 * mc + 1:2 * mc + 2]), inc=vco3)

    W(A, "vector", vlam, 16)
    for mc in range(FC):
        OP(A, "vector", lambda e, mc=mc: e.tensor_copy(
            out=lam_fm[0:64, mc:mc + 1],
            in_=lamb_t[0:64, 2 * mc:2 * mc + 1]))
        OP(A, "vector", lambda e, mc=mc: e.tensor_copy(
            out=lam_fm[64:128, mc:mc + 1],
            in_=lamb_t[64:128, 2 * mc + 1:2 * mc + 2]))

    gwv_preB = gwv.n
    for mc in range(2):
        DMA(A, "gpsimd", wsl[:, mc, :],
            bass.AP(wv_in, mc * P * D, [[D, P], [1, D]]), gwv)

    for eng in ENGS:
        OP(A, eng, lambda e: e.nop(), inc=barA)

    with nc.Block() as blkA:
        PRG[0].run(blkA)
    stA.close()

    B = 1
    stB = ExitStack()
    EB = stB.enter_context
    state = EB(nc.sbuf_tensor("state", [P, FC, TOK], BF16))
    ybf = EB(nc.sbuf_tensor("ybf", [P, FC, TOK], BF16))
    ysqb = EB(nc.sbuf_tensor("ysqb", [P, 2, FC, TTS], BF16))
    a2y_sb = EB(nc.sbuf_tensor("a2y_sb", [P, TOK], F32))
    b2y_sb = EB(nc.sbuf_tensor("b2y_sb", [P, TOK], F32))
    gs_sb = EB(nc.sbuf_tensor("gs_sb", [P, 8, TTS], BF16))
    attev = EB(nc.sbuf_tensor("attev", [P, 2, TTS], F32))
    tmpB = EB(nc.sbuf_tensor("tmpB", [P, 2, TTS], F32))
    t2b = EB(nc.sbuf_tensor("t2b", [P, 2, TTS], F32))

    for eng in ENGS:
        W(B, eng, barA)

    pb = [PS[:, i * 512:(i + 1) * 512] for i in range(8)]

    def attn_stream(w_dr, cur_sem, cur0, prev_sem, prev_val,
                    base=None, skip=0):
        if base is None:
            base = gwv.n
        for mc in range(skip, FC):
            if mc < 2:
                W(B, "gpsimd", prev_sem, prev_val)
            else:
                W(B, "gpsimd", cur_sem, cur0 + mc - 1)
            DMA(B, "gpsimd", wsl[:, mc % 2, :],
                bass.AP(w_dr, mc * P * D, [[D, P], [1, D]]), gwv)
        return base

    gb = attn_stream(wv_in, peV, peV.n, barA, 0, base=gwv_preB, skip=2)
    for mc in range(FC):
        par = mc % 2
        W(B, "tensor", gwv, gb + 16 * (mc + 1))
        if mc >= 2:
            W(B, "tensor", vsc, mc - 1)
        for kc in range(FC):
            for tt in range(NTT):
                OP(B, "tensor", lambda e, mc=mc, kc=kc, tt=tt, par=par: e.matmul(
                    pb[par * 4 + tt], wsl[:, par, kc * P:(kc + 1) * P],
                    xn[:, kc, tt * TTS:(tt + 1) * TTS],
                    start=(kc == 0), stop=(kc == FC - 1)),
                    inc=(peV if (kc == FC - 1 and tt == NTT - 1) else None))
        W(B, "vector", peV)
        OP(B, "vector", lambda e, mc=mc, par=par: e.tensor_tensor_scan(
            state[:, mc, :],
            bass.AP(lam_fm, mc, [[FC, P], [0, TOK]]),
            PS[:, par * 2048:(par + 1) * 2048],
            carry_fm[:, mc:mc + 1], ALU.mult, ALU.add), inc=vsc)

    peQ0, vy0, peV_tot = peQ.n, vy.n, peV.n
    gb = attn_stream(wq_in, peQ, peQ0, peV, peV_tot)
    W(B, "tensor", vsc)
    for mc in range(FC):
        par = mc % 2
        W(B, "tensor", gwv, gb + 16 * (mc + 1))
        if mc >= 2:
            W(B, "tensor", vy, vy0 + mc - 1)
        for kc in range(FC):
            for tt in range(NTT):
                OP(B, "tensor", lambda e, mc=mc, kc=kc, tt=tt, par=par: e.matmul(
                    pb[par * 4 + tt], wsl[:, par, kc * P:(kc + 1) * P],
                    xn[:, kc, tt * TTS:(tt + 1) * TTS],
                    start=(kc == 0), stop=(kc == FC - 1)),
                    inc=(peQ if (kc == FC - 1 and tt == NTT - 1) else None))
        W(B, "vector", peQ)
        for tt in range(NTT):
            OP(B, "vector", lambda e, mc=mc, tt=tt, par=par: e.tensor_mul(
                ybf[:, mc, tt * TTS:(tt + 1) * TTS], pb[par * 4 + tt],
                state[:, mc, tt * TTS:(tt + 1) * TTS]),
                inc=(vy if tt == NTT - 1 else None))

    W(B, "tensor", vy)
    W(B, "scalar", vy)
    peSSy0c = peSSy.n
    for tt in range(NTT):
        par = tt % 2
        for mc in range(FC):
            if tt >= 2 and mc == 0:
                W(B, "scalar", peSSy, peSSy0c + tt - 1)
            OP(B, "scalar", lambda e, mc=mc, tt=tt, par=par: e.activation(
                out=ysqb[:, par, mc, :],
                in_=ybf[:, mc, tt * TTS:(tt + 1) * TTS], func=AF.Square),
                inc=(vys if mc == FC - 1 else None))
        W(B, "tensor", vys)
        if tt >= 2:
            W(B, "tensor", vyr1, tt - 1)
        sy = pb[par * 2]
        sy2 = pb[par * 2 + 1]
        for mc in range(FC):
            OP(B, "tensor", lambda e, S=sy, mc=mc, tt=tt: e.matmul(
                S[0:1, :], onesc_bf[:], ybf[:, mc, tt * TTS:(tt + 1) * TTS],
                start=(mc == 0), stop=(mc == FC - 1)))
            OP(B, "tensor", lambda e, S=sy2, mc=mc, par=par: e.matmul(
                S[0:1, :], onesc_bf[:], ysqb[:, par, mc, :],
                start=(mc == 0), stop=(mc == FC - 1)),
                inc=(peSSy if mc == FC - 1 else None))
        W(B, "vector", peSSy)
        mu_s = mrow[0:1, par * TTS:(par + 1) * TTS]
        a_s = arow[0:1, par * TTS:(par + 1) * TTS]
        b_s = brow[0:1, par * TTS:(par + 1) * TTS]
        OP(B, "vector", lambda e, S=sy, m=mu_s: e.tensor_scalar_mul(
            m, S[0:1, :], 1.0 / D))
        OP(B, "vector", lambda e, S=sy2, a=a_s: e.tensor_scalar_mul(
            a, S[0:1, :], 1.0 / D))
        OP(B, "vector", lambda e, m=mu_s: e.tensor_mul(tmpB[0:1, 0, :], m, m))
        OP(B, "vector", lambda e, a=a_s: e.tensor_sub(a, a, tmpB[0:1, 0, :]))
        OP(B, "vector", lambda e, a=a_s: e.tensor_scalar_add(a, a, EPS))
        OP(B, "vector", lambda e, a=a_s: e.reciprocal(out=a, in_=a), inc=vyr1)
        W(B, "scalar", vyr1)
        OP(B, "scalar", lambda e, a=a_s: e.activation(
            out=a, in_=a, func=AF.Sqrt), inc=syr)
        W(B, "vector", syr)
        OP(B, "vector", lambda e, m=mu_s, a=a_s, b=b_s: e.scalar_tensor_tensor(
            b, m, -1.0, a, ALU.mult, ALU.mult), inc=vyr2)
        A2 = pb[4 + par * 2]
        B2 = pb[5 + par * 2]
        W(B, "tensor", vyr2)
        if tt >= 2:
            W(B, "tensor", vab, tt - 1)
        OP(B, "tensor", lambda e, A2=A2, a=a_s: e.matmul(
            A2, onesr[:], a, start=True, stop=True))
        OP(B, "tensor", lambda e, B2=B2, b=b_s: e.matmul(
            B2, onesr[:], b, start=True, stop=True), inc=peABy)
        W(B, "vector", peABy)
        OP(B, "vector", lambda e, A2=A2, tt=tt: e.tensor_copy(
            out=a2y_sb[:, tt * TTS:(tt + 1) * TTS], in_=A2))
        OP(B, "vector", lambda e, B2=B2, tt=tt: e.tensor_copy(
            out=b2y_sb[:, tt * TTS:(tt + 1) * TTS], in_=B2), inc=vab)

    peG0, sgs0, vy20, peQ_tot = peG.n, sgs.n, vy2.n, peQ.n
    v2s0, s2v0 = v2s.n, s2v.n
    gb = attn_stream(wg_in, peG, peG0, peQ, peQ_tot)
    W(B, "tensor", vab)
    for mc in range(FC):
        par = mc % 2
        W(B, "tensor", gwv, gb + 16 * (mc + 1))
        if mc >= 2:
            W(B, "tensor", sgs, sgs0 + mc - 1)
        for kc in range(FC):
            for tt in range(NTT):
                OP(B, "tensor", lambda e, mc=mc, kc=kc, tt=tt, par=par: e.matmul(
                    pb[par * 4 + tt], wsl[:, par, kc * P:(kc + 1) * P],
                    xn[:, kc, tt * TTS:(tt + 1) * TTS],
                    start=(kc == 0), stop=(kc == FC - 1)),
                    inc=(peG if (kc == FC - 1 and tt == NTT - 1) else None))
        W(B, "scalar", peG)
        if mc >= 2:
            W(B, "scalar", vy2, vy20 + 4 * (mc - 1))
        for tt in range(NTT):
            OP(B, "scalar", lambda e, mc=mc, tt=tt, par=par: e.activation(
                out=gs_sb[:, par * 4 + tt, :], in_=pb[par * 4 + tt], func=AF.Silu),
                inc=(sgs if tt == NTT - 1 else None))
        W(B, "vector", sgs)

        def y2_pre(tt, mc=mc):
            idx = 4 * mc + tt
            sl = idx % 2
            if idx >= 2:
                W(B, "vector", s2v, s2v0 + idx - 1)
            OP(B, "vector", lambda e, mc=mc, tt=tt, sl=sl: e.tensor_mul(
                tmpB[:, sl, :], ybf[:, mc, tt * TTS:(tt + 1) * TTS],
                a2y_sb[:, tt * TTS:(tt + 1) * TTS]))
            OP(B, "vector", lambda e, tt=tt, sl=sl: e.tensor_add(
                tmpB[:, sl, :], tmpB[:, sl, :],
                b2y_sb[:, tt * TTS:(tt + 1) * TTS]), inc=v2s)
            W(B, "scalar", v2s, v2s0 + idx + 1)
            if idx >= 2:
                W(B, "scalar", vy2, vy20 + idx - 1)
            OP(B, "scalar", lambda e, mc=mc, sl=sl: e.activation(
                out=t2b[:, sl, :], in_=tmpB[:, sl, :], func=AF.Copy,
                scale=gam_t[:, mc:mc + 1]), inc=s2v)

        def y2_post(tt, mc=mc):
            idx = 4 * mc + tt
            sl = idx % 2
            W(B, "vector", s2v, s2v0 + idx + 1)
            OP(B, "vector", lambda e, mc=mc, tt=tt, sl=sl, par=mc % 2:
               e.scalar_tensor_tensor(
                   ybf[:, mc, tt * TTS:(tt + 1) * TTS], t2b[:, sl, :],
                   bet_t[:, mc:mc + 1], gs_sb[:, par * 4 + tt, :],
                   ALU.add, ALU.mult), inc=vy2)

        for tt in range(NTT):
            y2_pre(tt)
            if tt >= 1:
                y2_post(tt - 1)
        y2_post(NTT - 1)

    peWo0, peG_tot = peWo.n, peG.n
    gb = attn_stream(wo_in, peWo, peWo0, peG, peG_tot)
    W(B, "tensor", vy2)
    sp_i = 0
    for mc in range(FC):
        par = mc % 2
        W(B, "tensor", gwv, gb + 16 * (mc + 1))
        if mc >= 2:
            W(B, "tensor", sWo, (mc - 1) * NTT)
        for kc in range(FC):
            for tt in range(NTT):
                OP(B, "tensor", lambda e, mc=mc, kc=kc, tt=tt, par=par: e.matmul(
                    pb[par * 4 + tt], wsl[:, par, kc * P:(kc + 1) * P],
                    ybf[:, kc, tt * TTS:(tt + 1) * TTS],
                    start=(kc == 0), stop=(kc == FC - 1)),
                    inc=(peWo if (kc == FC - 1 and tt == NTT - 1) else None))
        W(B, "scalar", peWo)
        for tt in range(NTT):
            par2 = sp_i % 2
            if sp_i >= 2:
                W(B, "scalar", spl, 16 * (sp_i - 1))
            OP(B, "scalar", lambda e, tt=tt, par=par, par2=par2: e.copy(
                attev[:, par2, :], pb[par * 4 + tt]), inc=sWo)
            W(B, "sync", sWo)
            DMA(B, "sync",
                bass.AP(attn_sp, mc * TOK + tt * TTS, [[FC * TOK, P], [1, TTS]]),
                attev[:, par2, :], spl)
            sp_i += 1

    gw12_pre = gw12.n
    for mc in range(2):
        DMA(B, "gpsimd", w12b[:, mc, :],
            bass.AP(w12_in, mc * P * 2048, [[2048, P], [1, 2048]]), gw12)

    for eng in ENGS:
        OP(B, eng, lambda e: e.nop(), inc=barB)

    with nc.Block() as blkB:
        PRG[1].run(blkB)
    stB.close()
    mrow_cm.__exit__(None, None, None)
    brow_cm.__exit__(None, None, None)
    arow_cm.__exit__(None, None, None)

    C = 2
    stC = ExitStack()
    EC = stC.enter_context
    u_t = EC(nc.sbuf_tensor("u_t", [P, HC, 1024], BF16))
    h_t = EC(nc.sbuf_tensor("h_t", [P, HC, 1024], BF16))
    hbuf = EC(nc.sbuf_tensor("hbuf", [P, 3, 2048], BF16))
    u1sb = EC(nc.sbuf_tensor("u1sb", [P, 2, 1024], BF16))
    att_t = EC(nc.sbuf_tensor("att_t", [P, 2, TTS], F32))
    x_t = EC(nc.sbuf_tensor("x_t", [P, 2, TTS], F32))
    oB = EC(nc.sbuf_tensor("oB", [P, 2, TTS], F32))

    for eng in ENGS:
        W(C, eng, barB)
    W(C, "sync", spl)

    u1ps = [PS[:, 0:1024], PS[:, 1024:2048]]
    u2ps = [PS[:, 2048:3072], PS[:, 3072:4096]]
    hps = u1ps
    w3ps = u2ps

    jg_cell = [0]

    def emit_hloads(w_dr, pairs):
        for mc, hf in pairs:
            jg = jg_cell[0]
            jg_cell[0] += 1
            if jg >= 3:
                W(C, "gpsimd", peFHh, jg - 2)
            DMA(C, "gpsimd", hbuf[:, jg % 3, :],
                bass.AP(w_dr, mc * P * DH + hf * 2048,
                        [[DH, P], [1, 2048]]), gh)

    for p in range(2):
        TO = p * 1024

        vU0, sU0, peFU0 = vU.n, sU.n, peFU.n
        gw0 = gw12_pre if p == 0 else gw12.n
        h1_st = None
        W(C, "gpsimd", peFU)
        W(C, "tensor", vO)
        for mc in range(HC):
            par = mc % 2
            if mc == 2:
                h1_st = (jg_cell[0], gh.n)
                emit_hloads(h1_in, [(0, 0), (0, 1)])
            if not (p == 0 and mc < 2):
                if mc >= 2:
                    W(C, "gpsimd", peFU, peFU0 + mc - 1)
                DMA(C, "gpsimd", w12b[:, par, :],
                    bass.AP(w12_in, mc * P * 2048, [[2048, P], [1, 2048]]),
                    gw12)
            W(C, "tensor", gw12, gw0 + 16 * (mc + 1))
            if mc >= 2:
                W(C, "tensor", sU, sU0 + mc - 1)
                W(C, "tensor", vU, vU0 + mc - 1)
            for kc in range(FC):
                for ng in range(2):
                    OP(C, "tensor", lambda e, kc=kc, ng=ng, par=par, TO=TO:
                       e.matmul(
                           u1ps[par][:, ng * 512:(ng + 1) * 512],
                           w12b[:, par, kc * P:(kc + 1) * P],
                           xn[:, kc, TO + ng * 512:TO + (ng + 1) * 512],
                           start=(kc == 0), stop=(kc == FC - 1)))
                    OP(C, "tensor", lambda e, kc=kc, ng=ng, par=par, TO=TO:
                       e.matmul(
                           u2ps[par][:, ng * 512:(ng + 1) * 512],
                           w12b[:, par, D + kc * P: D + (kc + 1) * P],
                           xn[:, kc, TO + ng * 512:TO + (ng + 1) * 512],
                           start=(kc == 0), stop=(kc == FC - 1)),
                        inc=(peFU if (kc == FC - 1 and ng == 1) else None))
            W(C, "scalar", peFU)
            if mc >= 2:
                W(C, "scalar", vU, vU0 + mc - 1)
            OP(C, "scalar", lambda e, mc=mc, par=par: e.activation(
                out=u1sb[:, par, :], in_=u1ps[par][:], func=AF.Silu,
                scale=s1_t[:, mc:mc + 1]), inc=sU)
            W(C, "vector", sU)
            OP(C, "vector", lambda e, mc=mc, par=par: e.scalar_tensor_tensor(
                u_t[:, mc, :], u2ps[par][:], s2_t[:, mc:mc + 1], u1sb[:, par, :],
                ALU.mult, ALU.mult), inc=vU)

        def ffn_wide(w_dr, src_t, dst_t, scale_t, st=None):
            sH0 = sH.n
            if st is None:
                st = (jg_cell[0], gh.n)
                n_pre = 0
            else:
                n_pre = 2
            jg0, gh0 = st
            for mc in range(HC):
                for hf in range(2):
                    j = 2 * mc + hf
                    if j >= n_pre:
                        emit_hloads(w_dr, [(mc, hf)])
                par = mc % 2
                if mc >= 2:
                    W(C, "tensor", sH, sH0 + mc - 1)
                for kc in range(HC):
                    j = 2 * mc + kc // 16
                    if kc % 16 == 0:
                        W(C, "tensor", gh, gh0 + 16 * (j + 1))
                    for ng in range(2):
                        OP(C, "tensor",
                           lambda e, kc=kc, ng=ng, par=par, sl=(jg0 + j) % 3,
                           s=src_t: e.matmul(
                               hps[par][:, ng * 512:(ng + 1) * 512],
                               hbuf[:, sl, (kc % 16) * P:((kc % 16) + 1) * P],
                               s[:, kc, ng * 512:(ng + 1) * 512],
                               start=(kc == 0), stop=(kc == HC - 1)),
                           inc=(peFHh if (kc % 16 == 15 and ng == 1) else None))
                W(C, "scalar", peFHh)
                OP(C, "scalar", lambda e, mc=mc, par=par, d=dst_t, sc=scale_t:
                   e.activation(out=d[:, mc, :], in_=hps[par][:], func=AF.Gelu,
                                scale=sc[:, mc:mc + 1]), inc=sH)

        W(C, "tensor", vU)
        ffn_wide(h1_in, u_t, h_t, sh1_t, st=h1_st)
        W(C, "tensor", sH)
        ffn_wide(h2_in, h_t, u_t, sh2_t)

        vO0, sax0, sout0 = vO.n, sax.n, sout.n
        W(C, "tensor", sH)
        jg0w, gh0w = jg_cell[0], gh.n
        for mc in range(FC):
            emit_hloads(w3_in, [(mc, 0), (mc, 1)])
            par = mc % 2
            if mc >= 1:
                W(C, "sync", vO, vO0 + 2 * mc)
            for ng in range(2):
                DMA(C, "sync", att_t[:, ng, :],
                    bass.AP(attn_sp, mc * TOK + TO + ng * TTS,
                            [[FC * TOK, P], [1, TTS]]), sax)
                DMA(C, "sync", x_t[:, ng, :],
                    bass.AP(x_in, mc * TOK + TO + ng * TTS,
                            [[FC * TOK, P], [1, TTS]]), sax)
            if mc >= 2:
                W(C, "tensor", vO, vO0 + 2 * (mc - 1))
            for kc in range(HC):
                j = 2 * mc + kc // 16
                if kc % 16 == 0:
                    W(C, "tensor", gh, gh0w + 16 * (j + 1))
                for ng in range(2):
                    OP(C, "tensor", lambda e, kc=kc, ng=ng, par=par,
                       sl=(jg0w + j) % 3: e.matmul(
                           w3ps[par][:, ng * 512:(ng + 1) * 512],
                           hbuf[:, sl, (kc % 16) * P:((kc % 16) + 1) * P],
                           u_t[:, kc, ng * 512:(ng + 1) * 512],
                           start=(kc == 0), stop=(kc == HC - 1)),
                        inc=(peFHh if (kc % 16 == 15 and ng == 1) else None))
            W(C, "vector", peFHh)
            W(C, "vector", sax, sax0 + 64 * (mc + 1))
            if mc >= 1:
                W(C, "vector", sout, sout0 + 32 * mc)
            for ng in range(2):
                OP(C, "vector", lambda e, mc=mc, ng=ng, par=par:
                   e.scalar_tensor_tensor(
                       oB[:, ng, :], w3ps[par][:, ng * 512:(ng + 1) * 512],
                       s3_t[:, mc:mc + 1], att_t[:, ng, :], ALU.mult, ALU.add))
                OP(C, "vector", lambda e, ng=ng: e.tensor_add(
                    oB[:, ng, :], oB[:, ng, :], x_t[:, ng, :]), inc=vO)
                W(C, "sync", vO)
                DMA(C, "sync",
                    bass.AP(out_fm, mc * TOK + TO + ng * TTS,
                            [[FC * TOK, P], [1, TTS]]),
                    oB[:, ng, :], sout)

    W(C, "sync", sout)

    with nc.Block() as blkC:
        PRG[2].run(blkC)
    stC.close()
    xn_cm.__exit__(None, None, None)
    stC.close()
    ctx.close()
    nc.finalize()
    return nc


_NC_CACHE = None


def _get_nc():
    global _NC_CACHE
    if _NC_CACHE is None:
        _NC_CACHE = build_program()
    return _NC_CACHE


def _prep_weights(L0_attn_Wq, L0_attn_Wv, L0_attn_Wo, L0_attn_Wg,
                  L0_attn_beta, L0_attn_ln_gamma, L0_attn_ln_beta,
                  L0_ffn_w1_w, L0_ffn_w1_s, L0_ffn_w2_w, L0_ffn_w2_s,
                  L0_ffn_h1_w, L0_ffn_h1_s, L0_ffn_h2_w, L0_ffn_h2_s,
                  L0_ffn_w3_w, L0_ffn_w3_s):
    f32 = np.float32
    c = np.ascontiguousarray

    def tile_lhsT(WT, nmc, nkc):
        t = WT.reshape(nkc, P, nmc, P).transpose(2, 1, 0, 3)
        return c(t.reshape(nmc, P, nkc * P)).astype(f32)

    d = {}
    d["wq_in"] = tile_lhsT(np.asarray(L0_attn_Wq, f32).T, FC, FC)
    d["wv_in"] = tile_lhsT(np.asarray(L0_attn_Wv, f32).T, FC, FC)
    d["wg_in"] = tile_lhsT(np.asarray(L0_attn_Wg, f32).T, FC, FC)
    d["wo_in"] = tile_lhsT(np.asarray(L0_attn_Wo, f32).T, FC, FC)
    w1t = tile_lhsT(np.asarray(L0_ffn_w1_w, f32).T, HC, FC)
    w2t = tile_lhsT(np.asarray(L0_ffn_w2_w, f32).T, HC, FC)
    d["w12_in"] = c(np.concatenate([w1t, w2t], axis=2))
    d["h1_in"] = tile_lhsT(np.asarray(L0_ffn_h1_w, f32).T, HC, HC)
    d["h2_in"] = tile_lhsT(np.asarray(L0_ffn_h2_w, f32).T, HC, HC)
    d["w3_in"] = tile_lhsT(np.asarray(L0_ffn_w3_w, f32).T, FC, HC)
    d["s1_in"] = c(np.asarray(L0_ffn_w1_s, f32).reshape(HC, P).T)
    d["s2_in"] = c(np.asarray(L0_ffn_w2_s, f32).reshape(HC, P).T)
    d["sh1_in"] = c(np.asarray(L0_ffn_h1_s, f32).reshape(HC, P).T)
    d["sh2_in"] = c(np.asarray(L0_ffn_h2_s, f32).reshape(HC, P).T)
    d["s3_in"] = c(np.asarray(L0_ffn_w3_s, f32).reshape(FC, P).T)
    d["gam_in"] = c(np.asarray(L0_attn_ln_gamma, f32).reshape(FC, P).T)
    d["bet_in"] = c(np.asarray(L0_attn_ln_beta, f32).reshape(FC, P).T)
    d["beta_r"] = c(np.asarray(L0_attn_beta, f32).reshape(1, NH))
    return d


def kernel(L0_attn_x, L0_attn_Wq, L0_attn_Wv, L0_attn_Wo, L0_attn_Wg,
           L0_attn_beta, L0_attn_ln_gamma, L0_attn_ln_beta,
           L0_ffn_w1_w, L0_ffn_w1_s, L0_ffn_w2_w, L0_ffn_w2_s,
           L0_ffn_h1_w, L0_ffn_h1_s, L0_ffn_h2_w, L0_ffn_h2_s,
           L0_ffn_w3_w, L0_ffn_w3_s):
    x = np.asarray(L0_attn_x, np.float32)
    nc_ = _get_nc()
    wd = _prep_weights(L0_attn_Wq, L0_attn_Wv, L0_attn_Wo, L0_attn_Wg,
                       L0_attn_beta, L0_attn_ln_gamma, L0_attn_ln_beta,
                       L0_ffn_w1_w, L0_ffn_w1_s, L0_ffn_w2_w, L0_ffn_w2_s,
                       L0_ffn_h1_w, L0_ffn_h1_s, L0_ffn_h2_w, L0_ffn_h2_s,
                       L0_ffn_w3_w, L0_ffn_w3_s)

    in_maps = []
    for c_id in range(8):
        b, q = c_id // 4, c_id % 4
        xc = x[b, q * TOK:(q + 1) * TOK, :]
        x_fm = np.ascontiguousarray(
            xc.T.reshape(FC, P, TOK).transpose(1, 0, 2).reshape(P, FC * TOK))
        lo = q * TOK - PRI
        xpv = np.zeros((PRI, D), np.float32)
        if q > 0:
            xpv[max(0, -lo):] = x[b, max(0, lo):q * TOK, :]
        m = dict(wd)
        m["x_in"] = x_fm
        m["xp_in"] = xpv
        in_maps.append(m)

    res = run_bass_kernel_spmd(nc_, in_maps, core_ids=list(range(8)))

    out = np.empty_like(x)
    for c_id in range(8):
        b, q = c_id // 4, c_id % 4
        o = res.results[c_id]["out_fm"]
        out[b, q * TOK:(q + 1) * TOK, :] = \
            o.reshape(P, FC, TOK).transpose(1, 0, 2).reshape(D, TOK).T
    return out
